# version 11
# baseline (speedup 1.0000x reference)
import numpy as np
import concourse.bass as bass
import concourse.mybir as mybir
from concourse.bass_utils import run_bass_kernel_spmd

F32 = mybir.dt.float32
BF16 = mybir.dt.bfloat16
I32 = mybir.dt.int32
AF = mybir.ActivationFunctionType
ALU = mybir.AluOpType
AX = mybir.AxisListType

P = 128
BH = 32


class Sem:
    __slots__ = ("h", "c")

    def __init__(self, handle):
        self.h = handle
        self.c = 0


def build_nc(D=4096, T=1024, R=64, NC=256, LRH=2048, NJ_RUN=None):
    KC = D // P
    TT = T // P
    NJ = D // NC
    SC = KC * NC
    SPLIT = max(1, TT // 2)
    KPG = KC // SPLIT
    PC = KPG * NC
    RG = SC // BH
    RGP = RG // SPLIT
    THALF = min(512, T)
    NTH = T // THALF
    NHALF = max(1, D // LRH)
    KG4 = KC // 4

    nc = bass.Bass("TRN2", target_bir_lowering=False)

    x_sh = nc.dram_tensor("x_sh", [T, D], F32, kind="ExternalInput")
    wt_t = nc.dram_tensor("wt_t", [NJ * D, NC], F32, kind="ExternalInput")
    mk_t = nc.dram_tensor("mk_t", [NJ * D, NC], I32, kind="ExternalInput")
    wbt_t = nc.dram_tensor("wbt_t", [NJ * D, NC], F32, kind="ExternalInput")
    bias_r = nc.dram_tensor("bias_r", [1, D], F32, kind="ExternalInput")
    bbase_r = nc.dram_tensor("bbase_r", [1, D], F32, kind="ExternalInput")
    A_r = nc.dram_tensor("A_r", [R, D], F32, kind="ExternalInput")
    WL_r = nc.dram_tensor("WL_r", [R, D], F32, kind="ExternalInput")
    B_r = nc.dram_tensor("B_r", [D, R], F32, kind="ExternalInput")
    WR_r = nc.dram_tensor("WR_r", [D, R], F32, kind="ExternalInput")
    out_sh = nc.dram_tensor("out_sh", [T, D], F32, kind="ExternalOutput")

    from contextlib import ExitStack
    with ExitStack() as ctx:
        E = ctx.enter_context

        def sem(name):
            return Sem(E(nc.semaphore(name)))

        s_x = [sem(f"s_x{i}") for i in range(2)]
        s_lrA = sem("s_lrA")
        s_lrB = sem("s_lrB")
        s_win = [[sem(f"s_win{i}_{q}") for q in range(SPLIT)] for i in range(2)]
        s_min = [[sem(f"s_min{i}_{q}") for q in range(SPLIT)] for i in range(2)]
        s_wbL = sem("s_wbL")
        s_xr = [sem(f"s_xr{i}") for i in range(2)]
        s_ob = [sem(f"s_ob{i}") for i in range(2)]
        s_bb = [[sem(f"s_bb{m}_{i}") for i in range(2)] for m in range(2)]
        c_dv = sem("c_dv")
        c_ac = sem("c_ac")
        c_pe = sem("c_pe")
        c_gp = sem("c_gp")

        xnT_sb = [E(nc.sbuf_tensor(f"xnT_{kc}", [P, T], BF16)) for kc in range(KC)]
        aT_sb = E(nc.sbuf_tensor("aT_sb", [R, T], BF16))
        ApT_sb = E(nc.sbuf_tensor("ApT_sb", [P, KC * R], BF16))
        BpT_sb = E(nc.sbuf_tensor("BpT_sb", [R, D], BF16))
        ident = E(nc.sbuf_tensor("ident", [P, P], BF16))
        icol = E(nc.sbuf_tensor("icol", [P, P], F32))
        irow = E(nc.sbuf_tensor("irow", [P, 1], F32))
        bd_sb = E(nc.sbuf_tensor("bd_sb", [P, P], BF16))
        stS = E(nc.sbuf_tensor("stS", [P, 2], F32))
        stS2 = E(nc.sbuf_tensor("stS2", [P, 2], F32))
        stM = E(nc.sbuf_tensor("stM", [P, 2], F32))
        stMS = E(nc.sbuf_tensor("stMS", [P, 2], F32))
        stV = E(nc.sbuf_tensor("stV", [P, 2], F32))
        stSD = E(nc.sbuf_tensor("stSD", [P, 2], F32))
        stR = E(nc.sbuf_tensor("stR", [P, 2], F32))
        eps_sb = E(nc.sbuf_tensor("eps_sb", [P, 1], F32))
        bs_sb = [E(nc.sbuf_tensor(f"bs{i}", [P, RG], F32)) for i in range(2)]
        bs16_sb = [E(nc.sbuf_tensor(f"bs16_{i}", [P, RG], BF16)) for i in range(2)]
        ind_sb = [E(nc.sbuf_tensor(f"ind{i}", [P, RG], F32)) for i in range(2)]

        ps_s = [E(nc.psum_tensor(f"ps_s{i}", [P, 512], F32)) for i in range(2)]
        ps_f = [E(nc.psum_tensor(f"ps_f{i}", [P, 512], F32)) for i in range(2)]
        ps_a = E(nc.psum_tensor("ps_a", [R, THALF], F32))
        ps_bd = E(nc.psum_tensor("ps_bd", [P, 512], F32))
        ps_t = [E(nc.psum_tensor(f"ps_t{i}", [P, 1024], BF16)) for i in range(2)]

        ops = {k: [] for k in ("sync", "gpsimd", "vector", "scalar", "tensor")}

        def em(eng, fn):
            ops[eng].append(fn)

        def emv(fn):
            if c_dv.c > 0:
                ops["vector"].append(
                    lambda e, h=c_dv.h, v=c_dv.c: e.wait_ge(h, v))
            ops["vector"].append(fn)

        def dma(eng, out, in_, s, transpose=False):
            def f(e, out=out, in_=in_, transpose=transpose, h=s.h):
                e.dma_start(out=out, in_=in_, transpose=transpose).then_inc(h, 16)
            em(eng, f)
            s.c += 16
            return s.c

        def wait(eng, s, v):
            if v > 0:
                em(eng, lambda e, h=s.h, v=v: e.wait_ge(h, v))

        for b in range(4):
            for (c0, c1, val) in [(0, 32 * b, 0.0), (32 * b, 32 * b + 32, 1.0),
                                  (32 * b + 32, P, 0.0)]:
                if c0 == c1:
                    continue
                def f(g, b=b, c0=c0, c1=c1, val=val, h=c_gp.h):
                    g.memset(bd_sb[32 * b:32 * b + 32, c0:c1], val).then_inc(h, 1)
                em("gpsimd", f)
                c_gp.c += 1
        em("gpsimd", lambda g, h=c_gp.h: g.iota(
            icol[:, :], pattern=[[1, P]], base=0, channel_multiplier=0,
            allow_small_or_imprecise_dtypes=True).then_inc(h, 1))
        em("gpsimd", lambda g, h=c_gp.h: g.iota(
            irow[:, :], pattern=[[0, 1]], base=0, channel_multiplier=1,
            allow_small_or_imprecise_dtypes=True).then_inc(h, 1))
        c_gp.c += 2
        em("gpsimd", lambda g, h=c_gp.h: g.memset(eps_sb[:, :], 1e-5).then_inc(h, 1))
        c_gp.c += 1
        gp_consts = c_gp.c

        wait("vector", c_gp, gp_consts)
        emv(lambda v, h=c_dv.h: v.tensor_scalar(
            ident[:, :], icol[:, :], irow[:, :], None,
            ALU.is_equal).then_inc(h, 1))
        c_dv.c += 1
        dv_ident = c_dv.c

        em("scalar", lambda sc: sc.preload_act_table(AF.Sqrt))

        with ExitStack() as p0:
            E0 = p0.enter_context
            x_t = [E0(nc.sbuf_tensor(f"x_t{i}", [P, D], F32)) for i in range(2)]
            xn_t = [E0(nc.sbuf_tensor(f"xn_t{i}", [P, D], BF16)) for i in range(2)]
            HD = D // NHALF
            TPH = (D // P) // NHALF
            a_f = E0(nc.sbuf_tensor("a_f", [R, HD], F32))
            wl_f = E0(nc.sbuf_tensor("wl_f", [R, HD], F32))
            ap16 = E0(nc.sbuf_tensor("ap16", [R, HD], BF16))
            b_f = E0(nc.sbuf_tensor("b_f", [P, TPH * R], F32))
            wr_f = E0(nc.sbuf_tensor("wr_f", [P, TPH * R], F32))
            bsum = E0(nc.sbuf_tensor("bsum", [P, TPH * R], F32))
            bp16 = E0(nc.sbuf_tensor("bp16", [P, TPH * R], BF16))

            dv_apply = [0] * TT
            dv_var = [0] * TT
            ac_sq = [0] * TT
            pe_xT = [0] * TT
            dv_xTevac = {}
            pe_xgrp = {}
            for t in range(TT):
                p = t % 2
                if t >= 2:
                    wait("sync", c_dv, dv_apply[t - 2])
                dma("sync", x_t[p][:, :], x_sh[t * P:(t + 1) * P, :], s_x[p])
                xv = s_x[p].c
                wait("scalar", s_x[p], xv)
                if t >= 2:
                    wait("scalar", c_pe, pe_xT[t - 2])
                    wait("scalar", c_dv, dv_var[t - 2])
                def fsq(sc, p=p, h=c_ac.h):
                    sc.activation(xn_t[p][:, :], x_t[p][:, :], AF.Square,
                                  accum_out=stS2[:, p:p + 1]).then_inc(h, 1)
                em("scalar", fsq)
                c_ac.c += 1
                ac_sq[t] = c_ac.c
                wait("vector", s_x[p], xv)
                emv(lambda v, p=p, h=c_dv.h: v.reduce_sum(
                    stS[:, p:p + 1], x_t[p][:, :], axis=AX.X).then_inc(h, 1))
                c_dv.c += 1
                emv(lambda v, p=p, h=c_dv.h, iD=1.0 / D: v.tensor_scalar(
                    stM[:, p:p + 1], stS[:, p:p + 1], iD, None,
                    ALU.mult).then_inc(h, 1))
                c_dv.c += 1
                emv(lambda v, p=p, h=c_dv.h: v.tensor_scalar(
                    stMS[:, p:p + 1], stM[:, p:p + 1], stM[:, p:p + 1], None,
                    ALU.mult).then_inc(h, 1))
                c_dv.c += 1
                wait("vector", c_ac, ac_sq[t])
                emv(lambda v, p=p, h=c_dv.h, iD=1.0 / D: v.tensor_scalar(
                    stV[:, p:p + 1], stS2[:, p:p + 1], iD, stMS[:, p:p + 1],
                    ALU.mult, ALU.subtract).then_inc(h, 1))
                c_dv.c += 1
                dv_var[t] = c_dv.c
                wait("scalar", c_dv, dv_var[t])
                if t == 0:
                    wait("scalar", c_gp, gp_consts)
                em("scalar", lambda sc, p=p, h=c_ac.h: sc.activation(
                    stSD[:, p:p + 1], stV[:, p:p + 1], AF.Sqrt,
                    bias=eps_sb[:, 0:1]).then_inc(h, 1))
                c_ac.c += 1
                ac_std = c_ac.c
                wait("vector", c_ac, ac_std)
                emv(lambda v, p=p, h=c_dv.h: v.reciprocal(
                    stR[:, p:p + 1], stSD[:, p:p + 1]).then_inc(h, 1))
                c_dv.c += 1
                if t >= 2:
                    wait("vector", c_pe, pe_xT[t - 2])
                def fap(v, p=p, h=c_dv.h):
                    v.tensor_scalar(xn_t[p][:, :], x_t[p][:, :],
                                    stM[:, p:p + 1], stR[:, p:p + 1],
                                    ALU.subtract, ALU.mult).then_inc(h, 1)
                emv(fap)
                c_dv.c += 1
                dv_apply[t] = c_dv.c
                wait("tensor", c_dv, dv_apply[t])
                if t == 0:
                    wait("tensor", c_dv, dv_ident)
                for g in range(KG4):
                    gg = t * KG4 + g
                    bank = gg % 2
                    if gg >= 2:
                        wait("tensor", c_dv, dv_xTevac[gg - 2])
                    for q in range(4):
                        kc = g * 4 + q
                        def ftx(te, kc=kc, q=q, p=p, bank=bank, h3=c_pe.h,
                                last=(q == 3)):
                            ins = te.transpose(ps_t[bank][:, q * P:(q + 1) * P],
                                               xn_t[p][:, kc * P:(kc + 1) * P],
                                               ident[:, :])
                            if last:
                                ins.then_inc(h3, 1)
                        em("tensor", ftx)
                    c_pe.c += 1
                    pe_xgrp[gg] = c_pe.c
                    wait("vector", c_pe, pe_xgrp[gg])
                    for q in range(4):
                        kc = g * 4 + q
                        emv(lambda v, kc=kc, q=q, t=t, bank=bank, h2=c_dv.h:
                            v.tensor_copy(
                                xnT_sb[kc][:, t * P:(t + 1) * P],
                                ps_t[bank][:, q * P:(q + 1) * P]).then_inc(h2, 1))
                        c_dv.c += 1
                    dv_xTevac[gg] = c_dv.c
                pe_xT[t] = c_pe.c

            gg_last = TT * KG4 - 1

            dv_tevac = {}
            for h in range(NHALF):
                if h > 0:
                    wait("gpsimd", c_dv, dv_ap_add)
                dma("gpsimd", a_f[:, :], A_r[:, h * HD:(h + 1) * HD], s_lrA)
                dma("gpsimd", wl_f[:, :], WL_r[:, h * HD:(h + 1) * HD], s_lrA)
                wait("vector", s_lrA, s_lrA.c)
                emv(lambda v, h2=c_dv.h: v.tensor_tensor(
                    ap16[:, :], a_f[:, :], wl_f[:, :], ALU.add).then_inc(h2, 1))
                c_dv.c += 1
                dv_ap_add = c_dv.c
                wait("tensor", c_dv, dv_ap_add)
                if h == 0:
                    wait("tensor", c_dv, dv_xTevac[gg_last])
                for i in range(HD // P):
                    kc = h * (HD // P) + i
                    sl = kc % 4
                    if kc >= 1:
                        wait("tensor", c_dv, dv_tevac[kc - 1])
                    def ftr(te, i=i, sl=sl, h3=c_pe.h):
                        te.transpose(ps_t[0][:, sl * R:(sl + 1) * R],
                                     ap16[:, i * P:(i + 1) * P],
                                     ident[0:R, 0:R]).then_inc(h3, 1)
                    em("tensor", ftr)
                    c_pe.c += 1
                    wait("vector", c_pe, c_pe.c)
                    emv(lambda v, kc=kc, sl=sl, h2=c_dv.h: v.tensor_copy(
                        ApT_sb[:, kc * R:(kc + 1) * R],
                        ps_t[0][:, sl * R:(sl + 1) * R]).then_inc(h2, 1))
                    c_dv.c += 1
                    dv_tevac[kc] = c_dv.c

            dv_tevac2 = {}
            for h in range(NHALF):
                if h > 0:
                    wait("gpsimd", c_dv, dv_bp16)
                for i in range(TPH):
                    a = h * TPH + i
                    dma("gpsimd", b_f[:, i * R:(i + 1) * R],
                        B_r[a * P:(a + 1) * P, :], s_lrB)
                    dma("gpsimd", wr_f[:, i * R:(i + 1) * R],
                        WR_r[a * P:(a + 1) * P, :], s_lrB)
                wait("vector", s_lrB, s_lrB.c)
                emv(lambda v, h2=c_dv.h: v.tensor_tensor(
                    bsum[:, :], b_f[:, :], wr_f[:, :], ALU.add).then_inc(h2, 1))
                c_dv.c += 1
                emv(lambda v, h2=c_dv.h: v.tensor_scalar(
                    bp16[:, :], bsum[:, :], 0.25, None, ALU.mult).then_inc(h2, 1))
                c_dv.c += 1
                dv_bp16 = c_dv.c
                wait("tensor", c_dv, dv_bp16)
                for i in range(TPH):
                    a = h * TPH + i
                    sl = a % 4
                    if a >= 1:
                        wait("tensor", c_dv, dv_tevac2[a - 1])
                    if a == 0 and dv_tevac:
                        wait("tensor", c_dv, max(dv_tevac.values()))
                    def ftr2(te, i=i, sl=sl, h3=c_pe.h):
                        te.transpose(ps_t[0][0:R, sl * P:(sl + 1) * P],
                                     bp16[:, i * R:(i + 1) * R],
                                     ident[:, :]).then_inc(h3, 1)
                    em("tensor", ftr2)
                    c_pe.c += 1
                    wait("vector", c_pe, c_pe.c)
                    emv(lambda v, a=a, sl=sl, h2=c_dv.h: v.tensor_copy(
                        BpT_sb[:, a * P:(a + 1) * P],
                        ps_t[0][0:R, sl * P:(sl + 1) * P]).then_inc(h2, 1))
                    c_dv.c += 1
                    dv_tevac2[a] = c_dv.c

            dv_p0 = c_dv.c
            ac_p0 = c_ac.c
            pe_p0 = c_pe.c

        dv_aT_evac = 0
        for th in range(NTH):
            if th > 0:
                wait("tensor", c_dv, dv_aT_evac)
            for kc in range(KC):
                def fa(te, kc=kc, th=th, last=(kc == KC - 1), h3=c_pe.h):
                    ins = te.matmul(
                        ps_a[:, :],
                        ApT_sb[:, kc * R:(kc + 1) * R],
                        xnT_sb[kc][:, th * THALF:(th + 1) * THALF],
                        start=(kc == 0), stop=last)
                    if last:
                        ins.then_inc(h3, 1)
                em("tensor", fa)
            c_pe.c += 1
            wait("vector", c_pe, c_pe.c)
            emv(lambda v, th=th, h2=c_dv.h: v.tensor_copy(
                aT_sb[:, th * THALF:(th + 1) * THALF], ps_a[:, :]).then_inc(h2, 1))
            c_dv.c += 1
            dv_aT_evac = c_dv.c
        dv_aT_done = c_dv.c

        wT_sb = [E(nc.sbuf_tensor(f"wT_{i}", [P, SC], BF16)) for i in range(2)]
        w_in = [E(nc.sbuf_tensor(f"w_in{i}", [P, SC], BF16)) for i in range(2)]
        m_in = [E(nc.sbuf_tensor(f"m_in{i}", [P, SC], BF16)) for i in range(2)]
        s_sb = E(nc.sbuf_tensor("s_sb", [P, TT * NC], F32))
        fg_sb = [E(nc.sbuf_tensor(f"fg_sb{i}", [P, NC], F32)) for i in range(2)]
        g_sb = [E(nc.sbuf_tensor(f"g_sb{i}", [P, NC], F32)) for i in range(2)]
        p_sb = [E(nc.sbuf_tensor(f"p_sb{i}", [P, NC], F32)) for i in range(2)]
        xr_sb = [E(nc.sbuf_tensor(f"xr_sb{i}", [P, NC], F32)) for i in range(2)]
        ob_sb = [E(nc.sbuf_tensor(f"ob_sb{i}", [P, NC], F32)) for i in range(2)]
        bb_sb = [[E(nc.sbuf_tensor(f"bb_sb{m}_{i}", [P, NC], F32)) for i in range(2)]
                 for m in range(2)]

        for eng in ("sync", "gpsimd", "vector", "scalar", "tensor"):
            wait(eng, c_dv, dv_p0)
            wait(eng, c_ac, ac_p0)
            wait(eng, c_pe, pe_p0)

        em("scalar", lambda sc: sc.preload_act_table(AF.Silu))

        dv_mul = {}
        dv_bs16 = {}
        dv_min = {}
        pe_bd = {}
        wb_cnt = {}
        min_cnt = {}
        win_cnt = {}
        pe_chunk_done = {}

        def tiled_rows(src, j, q):
            r0 = j * D + q * KPG * P
            return src[r0:r0 + KPG * P, :].rearrange("(kc p) n -> p kc n", p=P)

        def strip3(t_sb, q):
            return t_sb[:, q * PC:(q + 1) * PC].rearrange(
                "p (kc n) -> p kc n", n=NC)

        def emit_m_dmas(j):
            par = j % 2
            if j >= 2:
                wait("gpsimd", c_dv, dv_mul[j - 2])
            for q in range(SPLIT):
                dma("gpsimd", strip3(m_in[par], q), tiled_rows(mk_t, j, q),
                    s_min[par][q])
                min_cnt[(j, q)] = s_min[par][q].c

        def emit_w_dmas(j):
            par = j % 2
            for q in range(SPLIT):
                dma("gpsimd", strip3(w_in[par], q), tiled_rows(wt_t, j, q),
                    s_win[par][q])
                win_cnt[(j, q)] = s_win[par][q].c

        def emit_wb_dmas(j):
            g = 2 * j + 1
            if g >= 2:
                wait("gpsimd", c_pe, pe_chunk_done[g - 2])
            for q in range(SPLIT):
                dma("gpsimd", strip3(wT_sb[1], q), tiled_rows(wbt_t, j, q),
                    s_wbL)
            wb_cnt[j] = s_wbL.c

        def emit_reduce_piece(j, q):
            par = j % 2
            sl = j % 2
            wait("vector", s_min[par][q], min_cnt[(j, q)])
            def frd(v, par=par, sl=sl, q=q, h2=c_dv.h):
                v.reduce_sum(
                    bs_sb[sl][:, q * RGP:(q + 1) * RGP],
                    m_in[par][:, q * PC:(q + 1) * PC].rearrange(
                        "p (a b) -> p a b", b=BH),
                    axis=AX.X).then_inc(h2, 1)
            emv(frd)
            c_dv.c += 1

        def emit_bs16(j):
            sl = j % 2
            if j >= 2:
                wait("vector", c_pe, pe_bd[j - 2])
            emv(lambda v, sl=sl, h2=c_dv.h: v.tensor_copy(
                bs16_sb[sl][:, :], bs_sb[sl][:, :]).then_inc(h2, 1))
            c_dv.c += 1
            dv_bs16[j] = c_dv.c

        def emit_bd(j):
            sl = j % 2
            wait("tensor", c_dv, dv_bs16[j])
            if j >= 1:
                wait("tensor", c_dv, dv_min[j - 1])
            def fbd(te, sl=sl, h3=c_pe.h):
                te.matmul(ps_bd[:, sl * RG:(sl + 1) * RG],
                          bd_sb[:, :], bs16_sb[sl][:, :],
                          start=True, stop=True).then_inc(h3, 1)
            em("tensor", fbd)
            c_pe.c += 1
            pe_bd[j] = c_pe.c

        def emit_min(j):
            sl = j % 2
            wait("vector", c_pe, pe_bd[j])
            emv(lambda v, sl=sl, h2=c_dv.h: v.tensor_scalar(
                ind_sb[sl][:, :], ps_bd[:, sl * RG:(sl + 1) * RG],
                1.0, None, ALU.min).then_inc(h2, 1))
            c_dv.c += 1
            dv_min[j] = c_dv.c

        def emit_mul_piece(j, q):
            par = j % 2
            sl = j % 2
            wait("vector", s_win[par][q], win_cnt[(j, q)])
            if j >= 1:
                wait("vector", c_pe, pe_chunk_done[2 * (j - 1)])
            def fwm(v, par=par, sl=sl, q=q, h2=c_dv.h):
                ind_b = ind_sb[sl][:, q * RGP:(q + 1) * RGP].rearrange(
                    "p (a o) -> p a o", o=1).broadcast_to((P, RGP, BH))
                v.tensor_tensor(
                    wT_sb[0][:, q * PC:(q + 1) * PC].rearrange(
                        "p (a b) -> p a b", b=BH),
                    w_in[par][:, q * PC:(q + 1) * PC].rearrange(
                        "p (a b) -> p a b", b=BH),
                    ind_b, ALU.mult).then_inc(h2, 1)
            emv(fwm)
            c_dv.c += 1
            if q == SPLIT - 1:
                dv_mul[j] = c_dv.c

        dv_bb_lastuse = {}

        def emit_bias_load(m, j):
            slot = j % 2
            src = bias_r if m == 0 else bbase_r
            key = (m, slot, j - 2)
            if key in dv_bb_lastuse:
                wait("gpsimd", c_dv, dv_bb_lastuse[key])
            dma("gpsimd", bb_sb[m][slot][:, :],
                src[0:1, j * NC:(j + 1) * NC].broadcast_to((P, NC)), s_bb[m][slot])

        NJR = NJ if NJ_RUN is None else NJ_RUN
        emit_m_dmas(0)
        emit_w_dmas(0)
        if NJR > 1:
            emit_m_dmas(1)
            emit_w_dmas(1)
        emit_wb_dmas(0)
        for jj in range(min(2, NJR)):
            emit_bias_load(0, jj)
            emit_bias_load(1, jj)
        for q in range(SPLIT):
            emit_reduce_piece(0, q)
        emit_bs16(0)
        emit_bd(0)
        emit_min(0)
        for q in range(SPLIT):
            emit_mul_piece(0, q)
        if NJR > 1:
            for q in range(SPLIT):
                emit_reduce_piece(1, q)
            emit_bs16(1)
            emit_bd(1)
            emit_min(1)
        if NJR > 2:
            emit_m_dmas(2)
            emit_w_dmas(2)

        CH = [(m, j) for j in range(NJR) for m in (0, 1)]
        dv_sevac = {}
        dv_fevac = {}
        dv_epi = {}
        ac_silu_hist = {}
        dv_xr_lastuse = {}
        ob_round = [0, 0]
        xr_round = [0, 0]

        for g, (m, j) in enumerate(CH):
            if m == 0:
                wait("tensor", c_dv, dv_mul[j])
            else:
                wait("tensor", s_wbL, wb_cnt[j])
                if j == 0:
                    wait("tensor", c_dv, dv_aT_done)
            slot = j % 2
            for t in range(TT):
                pp = t % 2
                if m == 0:
                    ps = ps_s[pp]
                    prevkey = (t - 2, j) if t >= 2 else (TT - 2 + t, j - 1)
                    if prevkey in dv_sevac:
                        wait("tensor", c_dv, dv_sevac[prevkey])
                    if j > 0 and (t, j - 1) in dv_epi:
                        wait("tensor", c_dv, dv_epi[(t, j - 1)])
                else:
                    ps = ps_f[pp]
                    prevkey = (t - 2, j) if t >= 2 else (TT - 2 + t, j - 1)
                    if prevkey in dv_fevac:
                        wait("tensor", c_dv, dv_fevac[prevkey])
                    if t == min(3, TT - 1) and j + 2 < NJR:
                        emit_bd(j + 2)
                for kc in range(KC):
                    stop = (kc == KC - 1 and m == 0)
                    def fmm(te, kc=kc, t=t, ps=ps, strip=wT_sb[m],
                            start=(kc == 0), stop=stop, h3=c_pe.h):
                        ins = te.matmul(
                            ps[:, 0:NC],
                            xnT_sb[kc][:, t * P:(t + 1) * P],
                            strip[:, kc * NC:(kc + 1) * NC],
                            start=start, stop=stop)
                        if stop:
                            ins.then_inc(h3, 1)
                    em("tensor", fmm)
                if m == 1:
                    def flr(te, t=t, ps=ps, j=j, h3=c_pe.h):
                        te.matmul(ps[:, 0:NC],
                                  aT_sb[:, t * P:(t + 1) * P],
                                  BpT_sb[:, j * NC:(j + 1) * NC],
                                  start=False, stop=True).then_inc(h3, 1)
                    em("tensor", flr)
                c_pe.c += 1
                pe_tile = c_pe.c

                if m == 0:
                    wait("vector", c_pe, pe_tile)
                    wait("vector", s_bb[0][slot], s_bb[0][slot].c)
                    def fse(v, t=t, pp=pp, slot=slot, h2=c_dv.h):
                        v.tensor_tensor(s_sb[:, t * NC:(t + 1) * NC],
                                        ps_s[pp][:, 0:NC], bb_sb[0][slot][:, :],
                                        ALU.add).then_inc(h2, 1)
                    emv(fse)
                    c_dv.c += 1
                    dv_sevac[(t, j)] = c_dv.c
                    dv_bb_lastuse[(0, slot, j)] = c_dv.c
                    if j + 2 < NJR and t >= TT - SPLIT:
                        emit_reduce_piece(j + 2, t - (TT - SPLIT))
                        if t == TT - 1:
                            emit_bs16(j + 2)
                else:
                    epi = j * TT + t
                    pslot = epi % 2
                    key = (pslot, xr_round[pslot] - 1)
                    if key in dv_xr_lastuse:
                        wait("scalar", c_dv, dv_xr_lastuse[key])
                    dma("scalar", xr_sb[pslot][:, :],
                        x_sh[t * P:(t + 1) * P, j * NC:(j + 1) * NC], s_xr[pslot])
                    xrv = s_xr[pslot].c
                    xr_round[pslot] += 1
                    wait("vector", c_pe, pe_tile)
                    wait("vector", s_bb[1][slot], s_bb[1][slot].c)
                    if (epi - 2) in ac_silu_hist:
                        wait("vector", c_ac, ac_silu_hist[epi - 2])
                    def ffe(v, pp=pp, slot=slot, pslot=pslot, h2=c_dv.h):
                        v.tensor_tensor(fg_sb[pslot][:, :], ps_f[pp][:, 0:NC],
                                        bb_sb[1][slot][:, :], ALU.add).then_inc(h2, 1)
                    emv(ffe)
                    c_dv.c += 1
                    dv_fevac[(t, j)] = c_dv.c
                    dv_bb_lastuse[(1, slot, j)] = c_dv.c
                    wait("scalar", c_dv, dv_fevac[(t, j)])
                    em("scalar", lambda sc, pslot=pslot, h4=c_ac.h: sc.activation(
                        g_sb[pslot][:, :], fg_sb[pslot][:, :],
                        AF.Silu).then_inc(h4, 1))
                    c_ac.c += 1
                    ac_silu_hist[epi] = c_ac.c
                    wait("vector", c_ac, ac_silu_hist[epi])
                    def fp(v, t=t, pslot=pslot, h2=c_dv.h):
                        v.tensor_tensor(p_sb[pslot][:, :],
                                        s_sb[:, t * NC:(t + 1) * NC],
                                        g_sb[pslot][:, :], ALU.mult).then_inc(h2, 1)
                    emv(fp)
                    c_dv.c += 1
                    wait("vector", s_xr[pslot], xrv)
                    if ob_round[pslot] > 0:
                        wait("vector", s_ob[pslot], 16 * ob_round[pslot])
                    def fo(v, pslot=pslot, h2=c_dv.h):
                        v.tensor_tensor(ob_sb[pslot][:, :], p_sb[pslot][:, :],
                                        xr_sb[pslot][:, :], ALU.add).then_inc(h2, 1)
                    emv(fo)
                    c_dv.c += 1
                    dv_epi[(t, j)] = c_dv.c
                    dv_xr_lastuse[(pslot, xr_round[pslot] - 1)] = c_dv.c
                    wait("scalar", c_dv, dv_epi[(t, j)])
                    dma("scalar", out_sh[t * P:(t + 1) * P, j * NC:(j + 1) * NC],
                        ob_sb[pslot][:, :], s_ob[pslot])
                    ob_round[pslot] += 1
                    if t == min(4, TT - 1) and j + 2 < NJR:
                        emit_min(j + 2)
                    if t < SPLIT and j + 1 < NJR:
                        emit_mul_piece(j + 1, t)
            pe_chunk_done[g] = c_pe.c
            if m == 1:
                if j + 1 < NJR:
                    emit_wb_dmas(j + 1)
                if j + 2 < NJR:
                    emit_bias_load(0, j + 2)
                    emit_bias_load(1, j + 2)
                if j + 3 < NJR:
                    wait("gpsimd", c_dv, dv_mul[j + 1])
                    emit_w_dmas(j + 3)
                    emit_m_dmas(j + 3)

        wait("scalar", s_ob[0], 16 * ob_round[0])
        wait("scalar", s_ob[1], 16 * ob_round[1])
        wait("gpsimd", s_ob[0], 16 * ob_round[0])
        wait("gpsimd", s_ob[1], 16 * ob_round[1])

        return _finish(nc, ops)


def _finish(nc, ops):
    with nc.Block() as block:
        @block.sync
        def _(e):
            for f in ops["sync"]:
                f(e)

        @block.gpsimd
        def _(e):
            for f in ops["gpsimd"]:
                f(e)

        @block.vector
        def _(e):
            for f in ops["vector"]:
                f(e)

        @block.scalar
        def _(e):
            for f in ops["scalar"]:
                f(e)

        @block.tensor
        def _(e):
            for f in ops["tensor"]:
                f(e)
    return nc


def _tile_kmajor(a, NJ, NC, KC, P):
    N, K = a.shape
    return np.ascontiguousarray(
        a.reshape(NJ, NC, KC, P).transpose(0, 2, 3, 1)).reshape(NJ * KC * P, NC)


def kernel(L0_attn_data, L0_attn_mask, L0_attn_weight, L0_attn_bias,
           L0_ffn_W_base, L0_ffn_b_base, L0_ffn_A, L0_ffn_B, L0_ffn_WL,
           L0_ffn_WR):
    D, S, NCORES, NC = 4096, 8192, 8, 256
    P, KC, NJ = 128, D // 128, D // NC
    T = S // NCORES
    x = np.ascontiguousarray(np.asarray(L0_attn_data, np.float32).reshape(S, D))
    w = np.asarray(L0_attn_weight, np.float32)
    mk = np.asarray(L0_attn_mask, np.int32)
    wb = np.asarray(L0_ffn_W_base, np.float32)
    wt = _tile_kmajor(w, NJ, NC, KC, P)
    mkt = _tile_kmajor(mk, NJ, NC, KC, P)
    wbt = _tile_kmajor(wb, NJ, NC, KC, P)
    bias = np.ascontiguousarray(np.asarray(L0_attn_bias, np.float32).reshape(1, D))
    bb = np.ascontiguousarray(np.asarray(L0_ffn_b_base, np.float32).reshape(1, D))
    A = np.ascontiguousarray(np.asarray(L0_ffn_A, np.float32))
    WL = np.ascontiguousarray(np.asarray(L0_ffn_WL, np.float32))
    B = np.ascontiguousarray(np.asarray(L0_ffn_B, np.float32))
    WR = np.ascontiguousarray(np.asarray(L0_ffn_WR, np.float32))

    nc = build_nc(D=D, T=T, NC=NC)
    in_maps = []
    for c in range(NCORES):
        in_maps.append(dict(
            x_sh=x[c * T:(c + 1) * T], wt_t=wt, mk_t=mkt, wbt_t=wbt,
            bias_r=bias, bbase_r=bb, A_r=A, WL_r=WL, B_r=B, WR_r=WR))
    res = run_bass_kernel_spmd(nc, in_maps, core_ids=list(range(NCORES)))
    out = np.concatenate([res.results[c]["out_sh"] for c in range(NCORES)], axis=0)
    return out.reshape(1, S, D)


# version 12
# speedup vs baseline: 1.0431x; 1.0431x over previous
import numpy as np
import concourse.bass as bass
import concourse.mybir as mybir
from concourse.bass_utils import run_bass_kernel_spmd

F32 = mybir.dt.float32
BF16 = mybir.dt.bfloat16
I32 = mybir.dt.int32
AF = mybir.ActivationFunctionType
ALU = mybir.AluOpType
AX = mybir.AxisListType

P = 128
BH = 32


class Sem:
    __slots__ = ("h", "c")

    def __init__(self, handle):
        self.h = handle
        self.c = 0


def build_nc(D=4096, T=1024, R=64, NC=256, LRH=2048, NJ_RUN=None):
    KC = D // P
    TT = T // P
    NJ = D // NC
    SC = KC * NC
    SPLIT = max(1, TT // 2)
    KPG = KC // SPLIT
    PC = KPG * NC
    RG = SC // BH
    RGP = RG // SPLIT
    THALF = min(512, T)
    NTH = T // THALF
    NHALF = max(1, D // LRH)
    KG4 = KC // 4

    nc = bass.Bass("TRN2", target_bir_lowering=False)

    x_sh = nc.dram_tensor("x_sh", [T, D], F32, kind="ExternalInput")
    wt_t = nc.dram_tensor("wt_t", [NJ * D, NC], F32, kind="ExternalInput")
    mk_t = nc.dram_tensor("mk_t", [NJ * D, NC], I32, kind="ExternalInput")
    wbt_t = nc.dram_tensor("wbt_t", [NJ * D, NC], F32, kind="ExternalInput")
    bias_r = nc.dram_tensor("bias_r", [1, D], F32, kind="ExternalInput")
    bbase_r = nc.dram_tensor("bbase_r", [1, D], F32, kind="ExternalInput")
    A_r = nc.dram_tensor("A_r", [R, D], F32, kind="ExternalInput")
    WL_r = nc.dram_tensor("WL_r", [R, D], F32, kind="ExternalInput")
    B_r = nc.dram_tensor("B_r", [D, R], F32, kind="ExternalInput")
    WR_r = nc.dram_tensor("WR_r", [D, R], F32, kind="ExternalInput")
    out_sh = nc.dram_tensor("out_sh", [T, D], F32, kind="ExternalOutput")

    from contextlib import ExitStack
    with ExitStack() as ctx:
        E = ctx.enter_context

        def sem(name):
            return Sem(E(nc.semaphore(name)))

        s_x = [sem(f"s_x{i}") for i in range(2)]
        s_lrA = sem("s_lrA")
        s_lrB = sem("s_lrB")
        s_win = [sem(f"s_win{i}") for i in range(2)]
        s_min = [sem(f"s_min{i}") for i in range(2)]
        s_wbL = sem("s_wbL")
        s_xr = [sem(f"s_xr{i}") for i in range(2)]
        s_ob = [sem(f"s_ob{i}") for i in range(2)]
        s_bb = [[sem(f"s_bb{m}_{i}") for i in range(2)] for m in range(2)]
        c_dv = sem("c_dv")
        c_ac = sem("c_ac")
        c_pe = sem("c_pe")
        c_gp = sem("c_gp")

        xnTt_sb = [E(nc.sbuf_tensor(f"xnTt_{t}", [P, KC * P], BF16))
                   for t in range(TT)]
        aT_sb = E(nc.sbuf_tensor("aT_sb", [R, T], BF16))
        ApT_sb = E(nc.sbuf_tensor("ApT_sb", [P, KC * R], BF16))
        BpT_sb = E(nc.sbuf_tensor("BpT_sb", [R, D], BF16))
        ident = E(nc.sbuf_tensor("ident", [P, P], BF16))
        icol = E(nc.sbuf_tensor("icol", [P, P], F32))
        irow = E(nc.sbuf_tensor("irow", [P, 1], F32))
        bd_sb = E(nc.sbuf_tensor("bd_sb", [P, P], BF16))
        stS = E(nc.sbuf_tensor("stS", [P, 2], F32))
        stS2 = E(nc.sbuf_tensor("stS2", [P, 2], F32))
        stM = E(nc.sbuf_tensor("stM", [P, 2], F32))
        stMS = E(nc.sbuf_tensor("stMS", [P, 2], F32))
        stV = E(nc.sbuf_tensor("stV", [P, 2], F32))
        stSD = E(nc.sbuf_tensor("stSD", [P, 2], F32))
        stR = E(nc.sbuf_tensor("stR", [P, 2], F32))
        eps_sb = E(nc.sbuf_tensor("eps_sb", [P, 1], F32))
        bs_sb = [E(nc.sbuf_tensor(f"bs{i}", [P, RG], F32)) for i in range(2)]
        bs16_sb = [E(nc.sbuf_tensor(f"bs16_{i}", [P, RG], BF16)) for i in range(2)]
        ind_sb = [E(nc.sbuf_tensor(f"ind{i}", [P, RG], F32)) for i in range(2)]

        ps_s = [E(nc.psum_tensor(f"ps_s{i}", [P, 512], F32)) for i in range(2)]
        ps_f = [E(nc.psum_tensor(f"ps_f{i}", [P, 512], F32)) for i in range(2)]
        ps_a = E(nc.psum_tensor("ps_a", [R, THALF], F32))
        ps_bd = E(nc.psum_tensor("ps_bd", [P, 512], F32))
        ps_t = [E(nc.psum_tensor(f"ps_t{i}", [P, 1024], BF16)) for i in range(2)]

        ops = {k: [] for k in ("sync", "gpsimd", "vector", "scalar", "tensor")}

        def em(eng, fn):
            ops[eng].append(fn)

        def emv(fn):
            if c_dv.c > 0:
                ops["vector"].append(
                    lambda e, h=c_dv.h, v=c_dv.c: e.wait_ge(h, v))
            ops["vector"].append(fn)

        def dma(eng, out, in_, s, transpose=False):
            def f(e, out=out, in_=in_, transpose=transpose, h=s.h):
                e.dma_start(out=out, in_=in_, transpose=transpose).then_inc(h, 16)
            em(eng, f)
            s.c += 16
            return s.c

        def wait(eng, s, v):
            if v > 0:
                em(eng, lambda e, h=s.h, v=v: e.wait_ge(h, v))

        for b in range(4):
            for (c0, c1, val) in [(0, 32 * b, 0.0), (32 * b, 32 * b + 32, 1.0),
                                  (32 * b + 32, P, 0.0)]:
                if c0 == c1:
                    continue
                def f(g, b=b, c0=c0, c1=c1, val=val, h=c_gp.h):
                    g.memset(bd_sb[32 * b:32 * b + 32, c0:c1], val).then_inc(h, 1)
                em("gpsimd", f)
                c_gp.c += 1
        em("gpsimd", lambda g, h=c_gp.h: g.iota(
            icol[:, :], pattern=[[1, P]], base=0, channel_multiplier=0,
            allow_small_or_imprecise_dtypes=True).then_inc(h, 1))
        em("gpsimd", lambda g, h=c_gp.h: g.iota(
            irow[:, :], pattern=[[0, 1]], base=0, channel_multiplier=1,
            allow_small_or_imprecise_dtypes=True).then_inc(h, 1))
        c_gp.c += 2
        em("gpsimd", lambda g, h=c_gp.h: g.memset(eps_sb[:, :], 1e-5).then_inc(h, 1))
        c_gp.c += 1
        gp_consts = c_gp.c

        wait("vector", c_gp, gp_consts)
        emv(lambda v, h=c_dv.h: v.tensor_scalar(
            ident[:, :], icol[:, :], irow[:, :], None,
            ALU.is_equal).then_inc(h, 1))
        c_dv.c += 1
        dv_ident = c_dv.c

        em("scalar", lambda sc: sc.preload_act_table(AF.Sqrt))

        with ExitStack() as p0:
            E0 = p0.enter_context
            x_t = [E0(nc.sbuf_tensor(f"x_t{i}", [P, D], F32)) for i in range(2)]
            xn_t = [E0(nc.sbuf_tensor(f"xn_t{i}", [P, D], BF16)) for i in range(2)]
            HD = D // NHALF
            TPH = (D // P) // NHALF
            a_f = E0(nc.sbuf_tensor("a_f", [R, HD], F32))
            wl_f = E0(nc.sbuf_tensor("wl_f", [R, HD], F32))
            ap16 = E0(nc.sbuf_tensor("ap16", [R, HD], BF16))
            b_f = E0(nc.sbuf_tensor("b_f", [P, TPH * R], F32))
            wr_f = E0(nc.sbuf_tensor("wr_f", [P, TPH * R], F32))
            bsum = E0(nc.sbuf_tensor("bsum", [P, TPH * R], F32))
            bp16 = E0(nc.sbuf_tensor("bp16", [P, TPH * R], BF16))

            dv_apply = [0] * TT
            dv_var = [0] * TT
            ac_sq = [0] * TT
            pe_xT = [0] * TT
            dv_xTevac = {}
            pe_xgrp = {}
            for t in range(TT):
                p = t % 2
                if t >= 2:
                    wait("sync", c_dv, dv_apply[t - 2])
                dma("sync", x_t[p][:, :], x_sh[t * P:(t + 1) * P, :], s_x[p])
                xv = s_x[p].c
                wait("scalar", s_x[p], xv)
                if t >= 2:
                    wait("scalar", c_pe, pe_xT[t - 2])
                    wait("scalar", c_dv, dv_var[t - 2])
                def fsq(sc, p=p, h=c_ac.h):
                    sc.activation(xn_t[p][:, :], x_t[p][:, :], AF.Square,
                                  accum_out=stS2[:, p:p + 1]).then_inc(h, 1)
                em("scalar", fsq)
                c_ac.c += 1
                ac_sq[t] = c_ac.c
                wait("vector", s_x[p], xv)
                emv(lambda v, p=p, h=c_dv.h: v.reduce_sum(
                    stS[:, p:p + 1], x_t[p][:, :], axis=AX.X).then_inc(h, 1))
                c_dv.c += 1
                emv(lambda v, p=p, h=c_dv.h, iD=1.0 / D: v.tensor_scalar(
                    stM[:, p:p + 1], stS[:, p:p + 1], iD, None,
                    ALU.mult).then_inc(h, 1))
                c_dv.c += 1
                emv(lambda v, p=p, h=c_dv.h: v.tensor_scalar(
                    stMS[:, p:p + 1], stM[:, p:p + 1], stM[:, p:p + 1], None,
                    ALU.mult).then_inc(h, 1))
                c_dv.c += 1
                wait("vector", c_ac, ac_sq[t])
                emv(lambda v, p=p, h=c_dv.h, iD=1.0 / D: v.tensor_scalar(
                    stV[:, p:p + 1], stS2[:, p:p + 1], iD, stMS[:, p:p + 1],
                    ALU.mult, ALU.subtract).then_inc(h, 1))
                c_dv.c += 1
                dv_var[t] = c_dv.c
                wait("scalar", c_dv, dv_var[t])
                if t == 0:
                    wait("scalar", c_gp, gp_consts)
                em("scalar", lambda sc, p=p, h=c_ac.h: sc.activation(
                    stSD[:, p:p + 1], stV[:, p:p + 1], AF.Sqrt,
                    bias=eps_sb[:, 0:1]).then_inc(h, 1))
                c_ac.c += 1
                ac_std = c_ac.c
                wait("vector", c_ac, ac_std)
                emv(lambda v, p=p, h=c_dv.h: v.reciprocal(
                    stR[:, p:p + 1], stSD[:, p:p + 1]).then_inc(h, 1))
                c_dv.c += 1
                if t >= 2:
                    wait("vector", c_pe, pe_xT[t - 2])
                def fap(v, p=p, h=c_dv.h):
                    v.tensor_scalar(xn_t[p][:, :], x_t[p][:, :],
                                    stM[:, p:p + 1], stR[:, p:p + 1],
                                    ALU.subtract, ALU.mult).then_inc(h, 1)
                emv(fap)
                c_dv.c += 1
                dv_apply[t] = c_dv.c
                wait("tensor", c_dv, dv_apply[t])
                if t == 0:
                    wait("tensor", c_dv, dv_ident)
                for g in range(KG4):
                    gg = t * KG4 + g
                    bank = gg % 2
                    if gg >= 2:
                        wait("tensor", c_dv, dv_xTevac[gg - 2])
                    for q in range(4):
                        kc = g * 4 + q
                        def ftx(te, kc=kc, q=q, p=p, bank=bank, h3=c_pe.h,
                                last=(q == 3)):
                            ins = te.transpose(ps_t[bank][:, q * P:(q + 1) * P],
                                               xn_t[p][:, kc * P:(kc + 1) * P],
                                               ident[:, :])
                            if last:
                                ins.then_inc(h3, 1)
                        em("tensor", ftx)
                    c_pe.c += 1
                    pe_xgrp[gg] = c_pe.c
                    wait("vector", c_pe, pe_xgrp[gg])
                    emv(lambda v, g=g, t=t, bank=bank, h2=c_dv.h:
                        v.tensor_copy(
                            xnTt_sb[t][:, g * 4 * P:(g + 1) * 4 * P],
                            ps_t[bank][:, 0:4 * P]).then_inc(h2, 1))
                    c_dv.c += 1
                    dv_xTevac[gg] = c_dv.c
                pe_xT[t] = c_pe.c

            gg_last = TT * KG4 - 1

            dv_tevac = {}
            for h in range(NHALF):
                if h > 0:
                    wait("gpsimd", c_dv, dv_ap_add)
                dma("gpsimd", a_f[:, :], A_r[:, h * HD:(h + 1) * HD], s_lrA)
                dma("gpsimd", wl_f[:, :], WL_r[:, h * HD:(h + 1) * HD], s_lrA)
                wait("vector", s_lrA, s_lrA.c)
                emv(lambda v, h2=c_dv.h: v.tensor_tensor(
                    ap16[:, :], a_f[:, :], wl_f[:, :], ALU.add).then_inc(h2, 1))
                c_dv.c += 1
                dv_ap_add = c_dv.c
                wait("tensor", c_dv, dv_ap_add)
                if h == 0:
                    wait("tensor", c_dv, dv_xTevac[gg_last])
                for i in range(HD // P):
                    kc = h * (HD // P) + i
                    sl = kc % 4
                    if kc >= 1:
                        wait("tensor", c_dv, dv_tevac[kc - 1])
                    def ftr(te, i=i, sl=sl, h3=c_pe.h):
                        te.transpose(ps_t[0][:, sl * R:(sl + 1) * R],
                                     ap16[:, i * P:(i + 1) * P],
                                     ident[0:R, 0:R]).then_inc(h3, 1)
                    em("tensor", ftr)
                    c_pe.c += 1
                    wait("vector", c_pe, c_pe.c)
                    emv(lambda v, kc=kc, sl=sl, h2=c_dv.h: v.tensor_copy(
                        ApT_sb[:, kc * R:(kc + 1) * R],
                        ps_t[0][:, sl * R:(sl + 1) * R]).then_inc(h2, 1))
                    c_dv.c += 1
                    dv_tevac[kc] = c_dv.c

            dv_tevac2 = {}
            for h in range(NHALF):
                if h > 0:
                    wait("gpsimd", c_dv, dv_bp16)
                for i in range(TPH):
                    a = h * TPH + i
                    dma("gpsimd", b_f[:, i * R:(i + 1) * R],
                        B_r[a * P:(a + 1) * P, :], s_lrB)
                    dma("gpsimd", wr_f[:, i * R:(i + 1) * R],
                        WR_r[a * P:(a + 1) * P, :], s_lrB)
                wait("vector", s_lrB, s_lrB.c)
                emv(lambda v, h2=c_dv.h: v.tensor_tensor(
                    bsum[:, :], b_f[:, :], wr_f[:, :], ALU.add).then_inc(h2, 1))
                c_dv.c += 1
                emv(lambda v, h2=c_dv.h: v.tensor_scalar(
                    bp16[:, :], bsum[:, :], 0.25, None, ALU.mult).then_inc(h2, 1))
                c_dv.c += 1
                dv_bp16 = c_dv.c
                wait("tensor", c_dv, dv_bp16)
                for i in range(TPH):
                    a = h * TPH + i
                    sl = a % 4
                    if a >= 1:
                        wait("tensor", c_dv, dv_tevac2[a - 1])
                    if a == 0 and dv_tevac:
                        wait("tensor", c_dv, max(dv_tevac.values()))
                    def ftr2(te, i=i, sl=sl, h3=c_pe.h):
                        te.transpose(ps_t[0][0:R, sl * P:(sl + 1) * P],
                                     bp16[:, i * R:(i + 1) * R],
                                     ident[:, :]).then_inc(h3, 1)
                    em("tensor", ftr2)
                    c_pe.c += 1
                    wait("vector", c_pe, c_pe.c)
                    emv(lambda v, a=a, sl=sl, h2=c_dv.h: v.tensor_copy(
                        BpT_sb[:, a * P:(a + 1) * P],
                        ps_t[0][0:R, sl * P:(sl + 1) * P]).then_inc(h2, 1))
                    c_dv.c += 1
                    dv_tevac2[a] = c_dv.c

            dv_p0 = c_dv.c
            ac_p0 = c_ac.c
            pe_p0 = c_pe.c

        dv_aT_evac = 0
        TPB = THALF // P
        t0 = 0
        for t in range(TT):
            sl = t % TPB
            if sl == 0:
                t0 = t
                if t > 0:
                    wait("tensor", c_dv, dv_aT_evac)
            for kc in range(KC):
                def fa(te, kc=kc, t=t, sl=sl, last=(kc == KC - 1), h3=c_pe.h):
                    ins = te.matmul(
                        ps_a[:, sl * P:(sl + 1) * P],
                        ApT_sb[:, kc * R:(kc + 1) * R],
                        xnTt_sb[t][:, kc * P:(kc + 1) * P],
                        start=(kc == 0), stop=last)
                    if last:
                        ins.then_inc(h3, 1)
                em("tensor", fa)
            c_pe.c += 1
            if sl == TPB - 1 or t == TT - 1:
                wait("vector", c_pe, c_pe.c)
                emv(lambda v, t0=t0, t=t, sl=sl, h2=c_dv.h: v.tensor_copy(
                    aT_sb[:, t0 * P:(t + 1) * P],
                    ps_a[:, 0:(sl + 1) * P]).then_inc(h2, 1))
                c_dv.c += 1
                dv_aT_evac = c_dv.c
        dv_aT_done = c_dv.c

        wT_sb = [E(nc.sbuf_tensor(f"wT_{i}", [P, SC], BF16)) for i in range(2)]
        w_in = [E(nc.sbuf_tensor(f"w_in{i}", [P, SC], BF16)) for i in range(2)]
        m_in = [E(nc.sbuf_tensor(f"m_in{i}", [P, SC], BF16)) for i in range(2)]
        s_sb = E(nc.sbuf_tensor("s_sb", [P, TT * NC], F32))
        fg_sb = [E(nc.sbuf_tensor(f"fg_sb{i}", [P, NC], F32)) for i in range(2)]
        g_sb = [E(nc.sbuf_tensor(f"g_sb{i}", [P, NC], F32)) for i in range(2)]
        p_sb = [E(nc.sbuf_tensor(f"p_sb{i}", [P, NC], F32)) for i in range(2)]
        xr_sb = [E(nc.sbuf_tensor(f"xr_sb{i}", [P, NC], F32)) for i in range(2)]
        ob_sb = [E(nc.sbuf_tensor(f"ob_sb{i}", [P, NC], F32)) for i in range(2)]
        bb_sb = [[E(nc.sbuf_tensor(f"bb_sb{m}_{i}", [P, NC], F32)) for i in range(2)]
                 for m in range(2)]

        for eng in ("sync", "gpsimd", "vector", "scalar", "tensor"):
            wait(eng, c_dv, dv_p0)
            wait(eng, c_ac, ac_p0)
            wait(eng, c_pe, pe_p0)

        em("scalar", lambda sc: sc.preload_act_table(AF.Silu))

        dv_mul = {}
        dv_bs16 = {}
        dv_min = {}
        pe_bd = {}
        wb_cnt = {}
        min_cnt = {}
        win_cnt = {}
        pe_chunk_done = {}

        def tiled_rows(src, j, q):
            r0 = j * D + q * KPG * P
            return src[r0:r0 + KPG * P, :].rearrange("(kc p) n -> p kc n", p=P)

        def strip3(t_sb, q):
            return t_sb[:, q * PC:(q + 1) * PC].rearrange(
                "p (kc n) -> p kc n", n=NC)

        def emit_m_dmas(j):
            par = j % 2
            if j >= 2:
                wait("gpsimd", c_dv, dv_mul[j - 2])
            for q in range(SPLIT):
                dma("gpsimd", strip3(m_in[par], q), tiled_rows(mk_t, j, q),
                    s_min[par])
            min_cnt[j] = s_min[par].c

        def emit_w_dmas(j):
            par = j % 2
            for q in range(SPLIT):
                dma("gpsimd", strip3(w_in[par], q), tiled_rows(wt_t, j, q),
                    s_win[par])
            win_cnt[j] = s_win[par].c

        def emit_wb_dmas(j):
            g = 2 * j + 1
            if g >= 2:
                wait("gpsimd", c_pe, pe_chunk_done[g - 2])
            for q in range(SPLIT):
                dma("gpsimd", strip3(wT_sb[1], q), tiled_rows(wbt_t, j, q),
                    s_wbL)
            wb_cnt[j] = s_wbL.c

        def emit_reduce_piece(j, q):
            par = j % 2
            sl = j % 2
            wait("vector", s_min[par], min_cnt[j])
            def frd(v, par=par, sl=sl, q=q, h2=c_dv.h):
                v.reduce_sum(
                    bs_sb[sl][:, q * RGP:(q + 1) * RGP],
                    m_in[par][:, q * PC:(q + 1) * PC].rearrange(
                        "p (a b) -> p a b", b=BH),
                    axis=AX.X).then_inc(h2, 1)
            emv(frd)
            c_dv.c += 1

        def emit_bs16(j):
            sl = j % 2
            if j >= 2:
                wait("vector", c_pe, pe_bd[j - 2])
            emv(lambda v, sl=sl, h2=c_dv.h: v.tensor_copy(
                bs16_sb[sl][:, :], bs_sb[sl][:, :]).then_inc(h2, 1))
            c_dv.c += 1
            dv_bs16[j] = c_dv.c

        def emit_bd(j):
            sl = j % 2
            wait("tensor", c_dv, dv_bs16[j])
            if j >= 1:
                wait("tensor", c_dv, dv_min[j - 1])
            def fbd(te, sl=sl, h3=c_pe.h):
                te.matmul(ps_bd[:, sl * RG:(sl + 1) * RG],
                          bd_sb[:, :], bs16_sb[sl][:, :],
                          start=True, stop=True).then_inc(h3, 1)
            em("tensor", fbd)
            c_pe.c += 1
            pe_bd[j] = c_pe.c

        def emit_min(j):
            sl = j % 2
            wait("vector", c_pe, pe_bd[j])
            emv(lambda v, sl=sl, h2=c_dv.h: v.tensor_scalar(
                ind_sb[sl][:, :], ps_bd[:, sl * RG:(sl + 1) * RG],
                1.0, None, ALU.min).then_inc(h2, 1))
            c_dv.c += 1
            dv_min[j] = c_dv.c

        def emit_mul_piece(j, q):
            par = j % 2
            sl = j % 2
            wait("vector", s_win[par], win_cnt[j])
            if j >= 1:
                wait("vector", c_pe, pe_chunk_done[2 * (j - 1)])
            def fwm(v, par=par, sl=sl, q=q, h2=c_dv.h):
                ind_b = ind_sb[sl][:, q * RGP:(q + 1) * RGP].rearrange(
                    "p (a o) -> p a o", o=1).broadcast_to((P, RGP, BH))
                v.tensor_tensor(
                    wT_sb[0][:, q * PC:(q + 1) * PC].rearrange(
                        "p (a b) -> p a b", b=BH),
                    w_in[par][:, q * PC:(q + 1) * PC].rearrange(
                        "p (a b) -> p a b", b=BH),
                    ind_b, ALU.mult).then_inc(h2, 1)
            emv(fwm)
            c_dv.c += 1
            if q == SPLIT - 1:
                dv_mul[j] = c_dv.c

        dv_bb_lastuse = {}

        def emit_bias_load(m, j):
            slot = j % 2
            src = bias_r if m == 0 else bbase_r
            key = (m, slot, j - 2)
            if key in dv_bb_lastuse:
                wait("gpsimd", c_dv, dv_bb_lastuse[key])
            dma("gpsimd", bb_sb[m][slot][:, :],
                src[0:1, j * NC:(j + 1) * NC].broadcast_to((P, NC)), s_bb[m][slot])

        NJR = NJ if NJ_RUN is None else NJ_RUN
        emit_m_dmas(0)
        emit_w_dmas(0)
        if NJR > 1:
            emit_m_dmas(1)
            emit_w_dmas(1)
        emit_wb_dmas(0)
        for jj in range(min(2, NJR)):
            emit_bias_load(0, jj)
            emit_bias_load(1, jj)
        for q in range(SPLIT):
            emit_reduce_piece(0, q)
        emit_bs16(0)
        emit_bd(0)
        emit_min(0)
        for q in range(SPLIT):
            emit_mul_piece(0, q)
        if NJR > 1:
            for q in range(SPLIT):
                emit_reduce_piece(1, q)
            emit_bs16(1)
            emit_bd(1)
            emit_min(1)
        if NJR > 2:
            emit_m_dmas(2)
            emit_w_dmas(2)

        CH = [(m, j) for j in range(NJR) for m in (0, 1)]
        dv_sevac = {}
        dv_fevac = {}
        dv_epi = {}
        ac_silu_hist = {}
        dv_xr_lastuse = {}
        ob_round = [0, 0]
        xr_round = [0, 0]

        for g, (m, j) in enumerate(CH):
            if m == 0:
                wait("tensor", c_dv, dv_mul[j])
            else:
                wait("tensor", s_wbL, wb_cnt[j])
                if j == 0:
                    wait("tensor", c_dv, dv_aT_done)
            slot = j % 2
            for t in range(TT):
                pp = t % 2
                if m == 0:
                    ps = ps_s[pp]
                    prevkey = (t - 2, j) if t >= 2 else (TT - 2 + t, j - 1)
                    if prevkey in dv_sevac:
                        wait("tensor", c_dv, dv_sevac[prevkey])
                    if j > 0 and (t, j - 1) in dv_epi:
                        wait("tensor", c_dv, dv_epi[(t, j - 1)])
                else:
                    ps = ps_f[pp]
                    prevkey = (t - 2, j) if t >= 2 else (TT - 2 + t, j - 1)
                    if prevkey in dv_fevac:
                        wait("tensor", c_dv, dv_fevac[prevkey])
                    if t == 1 and j + 2 < NJR:
                        emit_bd(j + 2)
                for kc in range(KC):
                    stop = (kc == KC - 1 and m == 0)
                    def fmm(te, kc=kc, t=t, ps=ps, strip=wT_sb[m],
                            start=(kc == 0), stop=stop, h3=c_pe.h):
                        ins = te.matmul(
                            ps[:, 0:NC],
                            xnTt_sb[t][:, kc * P:(kc + 1) * P],
                            strip[:, kc * NC:(kc + 1) * NC],
                            start=start, stop=stop)
                        if stop:
                            ins.then_inc(h3, 1)
                    em("tensor", fmm)
                if m == 1:
                    def flr(te, t=t, ps=ps, j=j, h3=c_pe.h):
                        te.matmul(ps[:, 0:NC],
                                  aT_sb[:, t * P:(t + 1) * P],
                                  BpT_sb[:, j * NC:(j + 1) * NC],
                                  start=False, stop=True).then_inc(h3, 1)
                    em("tensor", flr)
                c_pe.c += 1
                pe_tile = c_pe.c

                if m == 0:
                    wait("vector", c_pe, pe_tile)
                    wait("vector", s_bb[0][slot], s_bb[0][slot].c)
                    def fse(v, t=t, pp=pp, slot=slot, h2=c_dv.h):
                        v.tensor_tensor(s_sb[:, t * NC:(t + 1) * NC],
                                        ps_s[pp][:, 0:NC], bb_sb[0][slot][:, :],
                                        ALU.add).then_inc(h2, 1)
                    emv(fse)
                    c_dv.c += 1
                    dv_sevac[(t, j)] = c_dv.c
                    dv_bb_lastuse[(0, slot, j)] = c_dv.c
                    if j + 2 < NJR and t >= TT - SPLIT:
                        emit_reduce_piece(j + 2, t - (TT - SPLIT))
                        if t == TT - 1:
                            emit_bs16(j + 2)
                else:
                    epi = j * TT + t
                    pslot = epi % 2
                    key = (pslot, xr_round[pslot] - 1)
                    if key in dv_xr_lastuse:
                        wait("scalar", c_dv, dv_xr_lastuse[key])
                    dma("scalar", xr_sb[pslot][:, :],
                        x_sh[t * P:(t + 1) * P, j * NC:(j + 1) * NC], s_xr[pslot])
                    xrv = s_xr[pslot].c
                    xr_round[pslot] += 1
                    wait("vector", c_pe, pe_tile)
                    wait("vector", s_bb[1][slot], s_bb[1][slot].c)
                    if (epi - 2) in ac_silu_hist:
                        wait("vector", c_ac, ac_silu_hist[epi - 2])
                    def ffe(v, pp=pp, slot=slot, pslot=pslot, h2=c_dv.h):
                        v.tensor_tensor(fg_sb[pslot][:, :], ps_f[pp][:, 0:NC],
                                        bb_sb[1][slot][:, :], ALU.add).then_inc(h2, 1)
                    emv(ffe)
                    c_dv.c += 1
                    dv_fevac[(t, j)] = c_dv.c
                    dv_bb_lastuse[(1, slot, j)] = c_dv.c
                    wait("scalar", c_dv, dv_fevac[(t, j)])
                    em("scalar", lambda sc, pslot=pslot, h4=c_ac.h: sc.activation(
                        g_sb[pslot][:, :], fg_sb[pslot][:, :],
                        AF.Silu).then_inc(h4, 1))
                    c_ac.c += 1
                    ac_silu_hist[epi] = c_ac.c
                    wait("vector", c_ac, ac_silu_hist[epi])
                    def fp(v, t=t, pslot=pslot, h2=c_dv.h):
                        v.tensor_tensor(p_sb[pslot][:, :],
                                        s_sb[:, t * NC:(t + 1) * NC],
                                        g_sb[pslot][:, :], ALU.mult).then_inc(h2, 1)
                    emv(fp)
                    c_dv.c += 1
                    wait("vector", s_xr[pslot], xrv)
                    if ob_round[pslot] > 0:
                        wait("vector", s_ob[pslot], 16 * ob_round[pslot])
                    def fo(v, pslot=pslot, h2=c_dv.h):
                        v.tensor_tensor(ob_sb[pslot][:, :], p_sb[pslot][:, :],
                                        xr_sb[pslot][:, :], ALU.add).then_inc(h2, 1)
                    emv(fo)
                    c_dv.c += 1
                    dv_epi[(t, j)] = c_dv.c
                    dv_xr_lastuse[(pslot, xr_round[pslot] - 1)] = c_dv.c
                    wait("scalar", c_dv, dv_epi[(t, j)])
                    dma("scalar", out_sh[t * P:(t + 1) * P, j * NC:(j + 1) * NC],
                        ob_sb[pslot][:, :], s_ob[pslot])
                    ob_round[pslot] += 1
                    if t == 1 and j + 2 < NJR:
                        emit_min(j + 2)
                    if t < SPLIT and j + 1 < NJR:
                        emit_mul_piece(j + 1, t)
            pe_chunk_done[g] = c_pe.c
            if m == 1:
                if j + 1 < NJR:
                    emit_wb_dmas(j + 1)
                if j + 2 < NJR:
                    emit_bias_load(0, j + 2)
                    emit_bias_load(1, j + 2)
                if j + 3 < NJR:
                    emit_m_dmas(j + 3)
                    emit_w_dmas(j + 3)

        wait("scalar", s_ob[0], 16 * ob_round[0])
        wait("scalar", s_ob[1], 16 * ob_round[1])
        wait("gpsimd", s_ob[0], 16 * ob_round[0])
        wait("gpsimd", s_ob[1], 16 * ob_round[1])

        return _finish(nc, ops)


def _finish(nc, ops):
    with nc.Block() as block:
        @block.sync
        def _(e):
            for f in ops["sync"]:
                f(e)

        @block.gpsimd
        def _(e):
            for f in ops["gpsimd"]:
                f(e)

        @block.vector
        def _(e):
            for f in ops["vector"]:
                f(e)

        @block.scalar
        def _(e):
            for f in ops["scalar"]:
                f(e)

        @block.tensor
        def _(e):
            for f in ops["tensor"]:
                f(e)
    return nc


def _tile_kmajor(a, NJ, NC, KC, P):
    N, K = a.shape
    return np.ascontiguousarray(
        a.reshape(NJ, NC, KC, P).transpose(0, 2, 3, 1)).reshape(NJ * KC * P, NC)


def kernel(L0_attn_data, L0_attn_mask, L0_attn_weight, L0_attn_bias,
           L0_ffn_W_base, L0_ffn_b_base, L0_ffn_A, L0_ffn_B, L0_ffn_WL,
           L0_ffn_WR):
    D, S, NCORES, NC = 4096, 8192, 8, 256
    P, KC, NJ = 128, D // 128, D // NC
    T = S // NCORES
    x = np.ascontiguousarray(np.asarray(L0_attn_data, np.float32).reshape(S, D))
    w = np.asarray(L0_attn_weight, np.float32)
    mk = np.asarray(L0_attn_mask, np.int32)
    wb = np.asarray(L0_ffn_W_base, np.float32)
    wt = _tile_kmajor(w, NJ, NC, KC, P)
    mkt = _tile_kmajor(mk, NJ, NC, KC, P)
    wbt = _tile_kmajor(wb, NJ, NC, KC, P)
    bias = np.ascontiguousarray(np.asarray(L0_attn_bias, np.float32).reshape(1, D))
    bb = np.ascontiguousarray(np.asarray(L0_ffn_b_base, np.float32).reshape(1, D))
    A = np.ascontiguousarray(np.asarray(L0_ffn_A, np.float32))
    WL = np.ascontiguousarray(np.asarray(L0_ffn_WL, np.float32))
    B = np.ascontiguousarray(np.asarray(L0_ffn_B, np.float32))
    WR = np.ascontiguousarray(np.asarray(L0_ffn_WR, np.float32))

    nc = build_nc(D=D, T=T, NC=NC)
    in_maps = []
    for c in range(NCORES):
        in_maps.append(dict(
            x_sh=x[c * T:(c + 1) * T], wt_t=wt, mk_t=mkt, wbt_t=wbt,
            bias_r=bias, bbase_r=bb, A_r=A, WL_r=WL, B_r=B, WR_r=WR))
    res = run_bass_kernel_spmd(nc, in_maps, core_ids=list(range(NCORES)))
    out = np.concatenate([res.results[c]["out_sh"] for c in range(NCORES)], axis=0)
    return out.reshape(1, S, D)



# version 13
# speedup vs baseline: 1.0932x; 1.0481x over previous
import numpy as np
import concourse.bass as bass
import concourse.mybir as mybir
from concourse.bass_utils import run_bass_kernel_spmd

F32 = mybir.dt.float32
BF16 = mybir.dt.bfloat16
I32 = mybir.dt.int32
AF = mybir.ActivationFunctionType
ALU = mybir.AluOpType
AX = mybir.AxisListType

P = 128
BH = 32


class Sem:
    __slots__ = ("h", "c")

    def __init__(self, handle):
        self.h = handle
        self.c = 0


def build_nc(D=4096, T=1024, R=64, NC=256, LRH=2048, NJ_RUN=None):
    KC = D // P
    TT = T // P
    NJ = D // NC
    SC = KC * NC
    SPLIT = max(1, TT // 2)
    KPG = KC // SPLIT
    PC = KPG * NC
    RG = SC // BH
    RGP = RG // SPLIT
    THALF = min(512, T)
    NTH = T // THALF
    NHALF = max(1, D // LRH)
    KG4 = KC // 4

    nc = bass.Bass("TRN2", target_bir_lowering=False)

    x_sh = nc.dram_tensor("x_sh", [T, D], F32, kind="ExternalInput")
    wt_t = nc.dram_tensor("wt_t", [NJ * D, NC], F32, kind="ExternalInput")
    mk_t = nc.dram_tensor("mk_t", [NJ * D, NC], I32, kind="ExternalInput")
    wbt_t = nc.dram_tensor("wbt_t", [NJ * D, NC], F32, kind="ExternalInput")
    bias_r = nc.dram_tensor("bias_r", [1, D], F32, kind="ExternalInput")
    bbase_r = nc.dram_tensor("bbase_r", [1, D], F32, kind="ExternalInput")
    A_r = nc.dram_tensor("A_r", [R, D], F32, kind="ExternalInput")
    WL_r = nc.dram_tensor("WL_r", [R, D], F32, kind="ExternalInput")
    B_r = nc.dram_tensor("B_r", [D, R], F32, kind="ExternalInput")
    WR_r = nc.dram_tensor("WR_r", [D, R], F32, kind="ExternalInput")
    out_sh = nc.dram_tensor("out_sh", [T, D], F32, kind="ExternalOutput")

    from contextlib import ExitStack
    with ExitStack() as ctx:
        E = ctx.enter_context

        def sem(name):
            return Sem(E(nc.semaphore(name)))

        s_x = [sem(f"s_x{i}") for i in range(2)]
        s_lrA = sem("s_lrA")
        s_lrB = sem("s_lrB")
        s_win = [sem(f"s_win{i}") for i in range(2)]
        s_min = [sem(f"s_min{i}") for i in range(2)]
        s_wbL = sem("s_wbL")
        s_xr = [sem(f"s_xr{i}") for i in range(2)]
        s_ob = [sem(f"s_ob{i}") for i in range(2)]
        s_bb = [[sem(f"s_bb{m}_{i}") for i in range(2)] for m in range(2)]
        c_dv = sem("c_dv")
        c_ac = sem("c_ac")
        c_pe = sem("c_pe")
        c_gp = sem("c_gp")

        xnTt_sb = [E(nc.sbuf_tensor(f"xnTt_{t}", [P, KC * P], BF16))
                   for t in range(TT)]
        aT_sb = E(nc.sbuf_tensor("aT_sb", [R, T], BF16))
        ApT_sb = E(nc.sbuf_tensor("ApT_sb", [P, KC * R], BF16))
        BpT_sb = E(nc.sbuf_tensor("BpT_sb", [R, D], BF16))
        ident = E(nc.sbuf_tensor("ident", [P, P], BF16))
        icol = E(nc.sbuf_tensor("icol", [P, P], F32))
        irow = E(nc.sbuf_tensor("irow", [P, 1], F32))
        bd_sb = E(nc.sbuf_tensor("bd_sb", [P, P], BF16))
        stS = E(nc.sbuf_tensor("stS", [P, 2], F32))
        stS2 = E(nc.sbuf_tensor("stS2", [P, 2], F32))
        stM = E(nc.sbuf_tensor("stM", [P, 2], F32))
        stMS = E(nc.sbuf_tensor("stMS", [P, 2], F32))
        stV = E(nc.sbuf_tensor("stV", [P, 2], F32))
        stSD = E(nc.sbuf_tensor("stSD", [P, 2], F32))
        stR = E(nc.sbuf_tensor("stR", [P, 2], F32))
        eps_sb = E(nc.sbuf_tensor("eps_sb", [P, 1], F32))
        bs_sb = [E(nc.sbuf_tensor(f"bs{i}", [P, RG], F32)) for i in range(2)]
        bs16_sb = [E(nc.sbuf_tensor(f"bs16_{i}", [P, RG], BF16)) for i in range(2)]
        ind_sb = [E(nc.sbuf_tensor(f"ind{i}", [P, RG], F32)) for i in range(2)]

        ps_s = [E(nc.psum_tensor(f"ps_s{i}", [P, 512], F32)) for i in range(2)]
        ps_f = [E(nc.psum_tensor(f"ps_f{i}", [P, 512], F32)) for i in range(2)]
        ps_a = E(nc.psum_tensor("ps_a", [R, THALF], F32))
        ps_bd = E(nc.psum_tensor("ps_bd", [P, 512], F32))
        ps_t = [E(nc.psum_tensor(f"ps_t{i}", [P, 1024], BF16)) for i in range(2)]

        ops = {k: [] for k in ("sync", "gpsimd", "vector", "scalar", "tensor")}

        def em(eng, fn):
            ops[eng].append(fn)

        def emv(fn):
            if c_dv.c > 0:
                ops["vector"].append(
                    lambda e, h=c_dv.h, v=c_dv.c: e.wait_ge(h, v))
            ops["vector"].append(fn)

        def ems(fn):
            if c_ac.c > 0:
                ops["scalar"].append(
                    lambda e, h=c_ac.h, v=c_ac.c: e.wait_ge(h, v))
            ops["scalar"].append(fn)

        def dma(eng, out, in_, s, transpose=False):
            def f(e, out=out, in_=in_, transpose=transpose, h=s.h):
                e.dma_start(out=out, in_=in_, transpose=transpose).then_inc(h, 16)
            em(eng, f)
            s.c += 16
            return s.c

        def wait(eng, s, v):
            if v > 0:
                em(eng, lambda e, h=s.h, v=v: e.wait_ge(h, v))

        for b in range(4):
            for (c0, c1, val) in [(0, 32 * b, 0.0), (32 * b, 32 * b + 32, 1.0),
                                  (32 * b + 32, P, 0.0)]:
                if c0 == c1:
                    continue
                def f(g, b=b, c0=c0, c1=c1, val=val, h=c_gp.h):
                    g.memset(bd_sb[32 * b:32 * b + 32, c0:c1], val).then_inc(h, 1)
                em("gpsimd", f)
                c_gp.c += 1
        em("gpsimd", lambda g, h=c_gp.h: g.iota(
            icol[:, :], pattern=[[1, P]], base=0, channel_multiplier=0,
            allow_small_or_imprecise_dtypes=True).then_inc(h, 1))
        em("gpsimd", lambda g, h=c_gp.h: g.iota(
            irow[:, :], pattern=[[0, 1]], base=0, channel_multiplier=1,
            allow_small_or_imprecise_dtypes=True).then_inc(h, 1))
        c_gp.c += 2
        em("gpsimd", lambda g, h=c_gp.h: g.memset(eps_sb[:, :], 1e-5).then_inc(h, 1))
        c_gp.c += 1
        gp_consts = c_gp.c

        wait("vector", c_gp, gp_consts)
        emv(lambda v, h=c_dv.h: v.tensor_scalar(
            ident[:, :], icol[:, :], irow[:, :], None,
            ALU.is_equal).then_inc(h, 1))
        c_dv.c += 1
        dv_ident = c_dv.c

        em("scalar", lambda sc: sc.preload_act_table(AF.Sqrt))

        with ExitStack() as p0:
            E0 = p0.enter_context
            x_t = [E0(nc.sbuf_tensor(f"x_t{i}", [P, D], F32)) for i in range(2)]
            xn_t = [E0(nc.sbuf_tensor(f"xn_t{i}", [P, D], BF16)) for i in range(2)]
            HD = D // NHALF
            TPH = (D // P) // NHALF
            a_f = E0(nc.sbuf_tensor("a_f", [R, HD], F32))
            wl_f = E0(nc.sbuf_tensor("wl_f", [R, HD], F32))
            ap16 = E0(nc.sbuf_tensor("ap16", [R, HD], BF16))
            b_f = E0(nc.sbuf_tensor("b_f", [P, TPH * R], F32))
            wr_f = E0(nc.sbuf_tensor("wr_f", [P, TPH * R], F32))
            bsum = E0(nc.sbuf_tensor("bsum", [P, TPH * R], F32))
            bp16 = E0(nc.sbuf_tensor("bp16", [P, TPH * R], BF16))

            dv_apply = [0] * TT
            dv_var = [0] * TT
            ac_sq = [0] * TT
            pe_xT = [0] * TT
            dv_xTevac = {}
            pe_xgrp = {}
            for t in range(TT):
                p = t % 2
                if t >= 2:
                    wait("sync", c_dv, dv_apply[t - 2])
                dma("sync", x_t[p][:, :], x_sh[t * P:(t + 1) * P, :], s_x[p])
                xv = s_x[p].c
                wait("scalar", s_x[p], xv)
                if t >= 2:
                    wait("scalar", c_pe, pe_xT[t - 2])
                    wait("scalar", c_dv, dv_var[t - 2])
                def fid(sc, p=p, h=c_ac.h):
                    sc.activation(xn_t[p][:, :], x_t[p][:, :], AF.Identity,
                                  accum_out=stS[:, p:p + 1]).then_inc(h, 1)
                ems(fid)
                c_ac.c += 1
                def fsq(sc, p=p, h=c_ac.h):
                    sc.activation(xn_t[p][:, :], x_t[p][:, :], AF.Square,
                                  accum_out=stS2[:, p:p + 1]).then_inc(h, 1)
                ems(fsq)
                c_ac.c += 1
                ac_sq[t] = c_ac.c
                wait("vector", c_ac, ac_sq[t])
                emv(lambda v, p=p, h=c_dv.h, iD=1.0 / D: v.tensor_scalar(
                    stM[:, p:p + 1], stS[:, p:p + 1], iD, None,
                    ALU.mult).then_inc(h, 1))
                c_dv.c += 1
                emv(lambda v, p=p, h=c_dv.h: v.tensor_scalar(
                    stMS[:, p:p + 1], stM[:, p:p + 1], stM[:, p:p + 1], None,
                    ALU.mult).then_inc(h, 1))
                c_dv.c += 1
                wait("vector", c_ac, ac_sq[t])
                emv(lambda v, p=p, h=c_dv.h, iD=1.0 / D: v.tensor_scalar(
                    stV[:, p:p + 1], stS2[:, p:p + 1], iD, stMS[:, p:p + 1],
                    ALU.mult, ALU.subtract).then_inc(h, 1))
                c_dv.c += 1
                dv_var[t] = c_dv.c
                wait("scalar", c_dv, dv_var[t])
                if t == 0:
                    wait("scalar", c_gp, gp_consts)
                ems(lambda sc, p=p, h=c_ac.h: sc.activation(
                    stSD[:, p:p + 1], stV[:, p:p + 1], AF.Sqrt,
                    bias=eps_sb[:, 0:1]).then_inc(h, 1))
                c_ac.c += 1
                ac_std = c_ac.c
                wait("vector", c_ac, ac_std)
                emv(lambda v, p=p, h=c_dv.h: v.reciprocal(
                    stR[:, p:p + 1], stSD[:, p:p + 1]).then_inc(h, 1))
                c_dv.c += 1
                if t >= 2:
                    wait("vector", c_pe, pe_xT[t - 2])
                def fap(v, p=p, h=c_dv.h):
                    v.tensor_scalar(xn_t[p][:, :], x_t[p][:, :],
                                    stM[:, p:p + 1], stR[:, p:p + 1],
                                    ALU.subtract, ALU.mult).then_inc(h, 1)
                emv(fap)
                c_dv.c += 1
                dv_apply[t] = c_dv.c
                wait("tensor", c_dv, dv_apply[t])
                if t == 0:
                    wait("tensor", c_dv, dv_ident)
                for g in range(KG4):
                    gg = t * KG4 + g
                    bank = gg % 2
                    if gg >= 2:
                        wait("tensor", c_dv, dv_xTevac[gg - 2])
                    for q in range(4):
                        kc = g * 4 + q
                        def ftx(te, kc=kc, q=q, p=p, bank=bank, h3=c_pe.h,
                                last=(q == 3)):
                            ins = te.transpose(ps_t[bank][:, q * P:(q + 1) * P],
                                               xn_t[p][:, kc * P:(kc + 1) * P],
                                               ident[:, :])
                            if last:
                                ins.then_inc(h3, 1)
                        em("tensor", ftx)
                    c_pe.c += 1
                    pe_xgrp[gg] = c_pe.c
                    wait("vector", c_pe, pe_xgrp[gg])
                    emv(lambda v, g=g, t=t, bank=bank, h2=c_dv.h:
                        v.tensor_copy(
                            xnTt_sb[t][:, g * 4 * P:(g + 1) * 4 * P],
                            ps_t[bank][:, 0:4 * P]).then_inc(h2, 1))
                    c_dv.c += 1
                    dv_xTevac[gg] = c_dv.c
                pe_xT[t] = c_pe.c

            gg_last = TT * KG4 - 1

            dv_tevac = {}
            for h in range(NHALF):
                if h > 0:
                    wait("gpsimd", c_dv, dv_ap_add)
                dma("gpsimd", a_f[:, :], A_r[:, h * HD:(h + 1) * HD], s_lrA)
                dma("gpsimd", wl_f[:, :], WL_r[:, h * HD:(h + 1) * HD], s_lrA)
                wait("vector", s_lrA, s_lrA.c)
                emv(lambda v, h2=c_dv.h: v.tensor_tensor(
                    ap16[:, :], a_f[:, :], wl_f[:, :], ALU.add).then_inc(h2, 1))
                c_dv.c += 1
                dv_ap_add = c_dv.c
                wait("tensor", c_dv, dv_ap_add)
                if h == 0:
                    wait("tensor", c_dv, dv_xTevac[gg_last])
                for i in range(HD // P):
                    kc = h * (HD // P) + i
                    sl = kc % 4
                    if kc >= 1:
                        wait("tensor", c_dv, dv_tevac[kc - 1])
                    def ftr(te, i=i, sl=sl, h3=c_pe.h):
                        te.transpose(ps_t[0][:, sl * R:(sl + 1) * R],
                                     ap16[:, i * P:(i + 1) * P],
                                     ident[0:R, 0:R]).then_inc(h3, 1)
                    em("tensor", ftr)
                    c_pe.c += 1
                    wait("vector", c_pe, c_pe.c)
                    emv(lambda v, kc=kc, sl=sl, h2=c_dv.h: v.tensor_copy(
                        ApT_sb[:, kc * R:(kc + 1) * R],
                        ps_t[0][:, sl * R:(sl + 1) * R]).then_inc(h2, 1))
                    c_dv.c += 1
                    dv_tevac[kc] = c_dv.c

            dv_tevac2 = {}
            for h in range(NHALF):
                if h > 0:
                    wait("gpsimd", c_dv, dv_bp16)
                for i in range(TPH):
                    a = h * TPH + i
                    dma("gpsimd", b_f[:, i * R:(i + 1) * R],
                        B_r[a * P:(a + 1) * P, :], s_lrB)
                    dma("gpsimd", wr_f[:, i * R:(i + 1) * R],
                        WR_r[a * P:(a + 1) * P, :], s_lrB)
                wait("vector", s_lrB, s_lrB.c)
                emv(lambda v, h2=c_dv.h: v.tensor_tensor(
                    bsum[:, :], b_f[:, :], wr_f[:, :], ALU.add).then_inc(h2, 1))
                c_dv.c += 1
                emv(lambda v, h2=c_dv.h: v.tensor_scalar(
                    bp16[:, :], bsum[:, :], 0.25, None, ALU.mult).then_inc(h2, 1))
                c_dv.c += 1
                dv_bp16 = c_dv.c
                wait("tensor", c_dv, dv_bp16)
                for i in range(TPH):
                    a = h * TPH + i
                    sl = a % 4
                    if a >= 1:
                        wait("tensor", c_dv, dv_tevac2[a - 1])
                    if a == 0 and dv_tevac:
                        wait("tensor", c_dv, max(dv_tevac.values()))
                    def ftr2(te, i=i, sl=sl, h3=c_pe.h):
                        te.transpose(ps_t[0][0:R, sl * P:(sl + 1) * P],
                                     bp16[:, i * R:(i + 1) * R],
                                     ident[:, :]).then_inc(h3, 1)
                    em("tensor", ftr2)
                    c_pe.c += 1
                    wait("vector", c_pe, c_pe.c)
                    emv(lambda v, a=a, sl=sl, h2=c_dv.h: v.tensor_copy(
                        BpT_sb[:, a * P:(a + 1) * P],
                        ps_t[0][0:R, sl * P:(sl + 1) * P]).then_inc(h2, 1))
                    c_dv.c += 1
                    dv_tevac2[a] = c_dv.c

            dv_p0 = c_dv.c
            ac_p0 = c_ac.c
            pe_p0 = c_pe.c

        dv_aT_evac = 0
        TPB = THALF // P
        t0 = 0
        for t in range(TT):
            sl = t % TPB
            if sl == 0:
                t0 = t
                if t > 0:
                    wait("tensor", c_dv, dv_aT_evac)
            for kc in range(KC):
                def fa(te, kc=kc, t=t, sl=sl, last=(kc == KC - 1), h3=c_pe.h):
                    ins = te.matmul(
                        ps_a[:, sl * P:(sl + 1) * P],
                        ApT_sb[:, kc * R:(kc + 1) * R],
                        xnTt_sb[t][:, kc * P:(kc + 1) * P],
                        start=(kc == 0), stop=last)
                    if last:
                        ins.then_inc(h3, 1)
                em("tensor", fa)
            c_pe.c += 1
            if sl == TPB - 1 or t == TT - 1:
                wait("vector", c_pe, c_pe.c)
                emv(lambda v, t0=t0, t=t, sl=sl, h2=c_dv.h: v.tensor_copy(
                    aT_sb[:, t0 * P:(t + 1) * P],
                    ps_a[:, 0:(sl + 1) * P]).then_inc(h2, 1))
                c_dv.c += 1
                dv_aT_evac = c_dv.c
        dv_aT_done = c_dv.c

        wT_sb = [E(nc.sbuf_tensor(f"wT_{i}", [P, SC], BF16)) for i in range(2)]
        w_in = [E(nc.sbuf_tensor(f"w_in{i}", [P, SC], BF16)) for i in range(2)]
        m_in = [E(nc.sbuf_tensor(f"m_in{i}", [P, SC], BF16)) for i in range(2)]
        s_sb = E(nc.sbuf_tensor("s_sb", [P, TT * NC], F32))
        fg_sb = [E(nc.sbuf_tensor(f"fg_sb{i}", [P, NC], F32)) for i in range(2)]
        g_sb = [E(nc.sbuf_tensor(f"g_sb{i}", [P, NC], F32)) for i in range(2)]
        p_sb = [E(nc.sbuf_tensor(f"p_sb{i}", [P, NC], F32)) for i in range(2)]
        xr_sb = [E(nc.sbuf_tensor(f"xr_sb{i}", [P, NC], F32)) for i in range(2)]
        ob_sb = [E(nc.sbuf_tensor(f"ob_sb{i}", [P, NC], F32)) for i in range(2)]
        bb_sb = [[E(nc.sbuf_tensor(f"bb_sb{m}_{i}", [P, NC], F32)) for i in range(2)]
                 for m in range(2)]

        for eng in ("sync", "gpsimd", "vector", "scalar", "tensor"):
            wait(eng, c_dv, dv_p0)
            wait(eng, c_ac, ac_p0)
            wait(eng, c_pe, pe_p0)

        em("scalar", lambda sc: sc.preload_act_table(AF.Silu))

        dv_mul = {}
        dv_bs16 = {}
        dv_min = {}
        pe_bd = {}
        wb_cnt = {}
        min_cnt = {}
        win_cnt = {}
        pe_chunk_done = {}

        def tiled_rows(src, j, q):
            r0 = j * D + q * KPG * P
            return src[r0:r0 + KPG * P, :].rearrange("(kc p) n -> p kc n", p=P)

        def strip3(t_sb, q):
            return t_sb[:, q * PC:(q + 1) * PC].rearrange(
                "p (kc n) -> p kc n", n=NC)

        def emit_m_dmas(j):
            par = j % 2
            if j >= 2:
                wait("gpsimd", c_dv, dv_mul[j - 2])
            for q in range(SPLIT):
                dma("gpsimd", strip3(m_in[par], q), tiled_rows(mk_t, j, q),
                    s_min[par])
            min_cnt[j] = s_min[par].c

        def emit_w_dmas(j):
            par = j % 2
            for q in range(SPLIT):
                dma("gpsimd", strip3(w_in[par], q), tiled_rows(wt_t, j, q),
                    s_win[par])
            win_cnt[j] = s_win[par].c

        def emit_wb_dmas(j):
            g = 2 * j + 1
            if g >= 2:
                wait("gpsimd", c_pe, pe_chunk_done[g - 2])
            for q in range(SPLIT):
                dma("gpsimd", strip3(wT_sb[1], q), tiled_rows(wbt_t, j, q),
                    s_wbL)
            wb_cnt[j] = s_wbL.c

        def emit_reduce_piece(j, q):
            par = j % 2
            sl = j % 2
            wait("vector", s_min[par], min_cnt[j])
            def frd(v, par=par, sl=sl, q=q, h2=c_dv.h):
                v.reduce_sum(
                    bs_sb[sl][:, q * RGP:(q + 1) * RGP],
                    m_in[par][:, q * PC:(q + 1) * PC].rearrange(
                        "p (a b) -> p a b", b=BH),
                    axis=AX.X).then_inc(h2, 1)
            emv(frd)
            c_dv.c += 1

        def emit_bs16(j):
            sl = j % 2
            if j >= 2:
                wait("vector", c_pe, pe_bd[j - 2])
            emv(lambda v, sl=sl, h2=c_dv.h: v.tensor_copy(
                bs16_sb[sl][:, :], bs_sb[sl][:, :]).then_inc(h2, 1))
            c_dv.c += 1
            dv_bs16[j] = c_dv.c

        def emit_bd(j):
            sl = j % 2
            wait("tensor", c_dv, dv_bs16[j])
            if j >= 1:
                wait("tensor", c_dv, dv_min[j - 1])
            def fbd(te, sl=sl, h3=c_pe.h):
                te.matmul(ps_bd[:, sl * RG:(sl + 1) * RG],
                          bd_sb[:, :], bs16_sb[sl][:, :],
                          start=True, stop=True).then_inc(h3, 1)
            em("tensor", fbd)
            c_pe.c += 1
            pe_bd[j] = c_pe.c

        def emit_min(j):
            sl = j % 2
            wait("vector", c_pe, pe_bd[j])
            emv(lambda v, sl=sl, h2=c_dv.h: v.tensor_scalar(
                ind_sb[sl][:, :], ps_bd[:, sl * RG:(sl + 1) * RG],
                1.0, None, ALU.min).then_inc(h2, 1))
            c_dv.c += 1
            dv_min[j] = c_dv.c

        def emit_mul_piece(j, q):
            par = j % 2
            sl = j % 2
            wait("vector", s_win[par], win_cnt[j])
            if j >= 1:
                wait("vector", c_pe, pe_chunk_done[2 * (j - 1)])
            def fwm(v, par=par, sl=sl, q=q, h2=c_dv.h):
                ind_b = ind_sb[sl][:, q * RGP:(q + 1) * RGP].rearrange(
                    "p (a o) -> p a o", o=1).broadcast_to((P, RGP, BH))
                v.tensor_tensor(
                    wT_sb[0][:, q * PC:(q + 1) * PC].rearrange(
                        "p (a b) -> p a b", b=BH),
                    w_in[par][:, q * PC:(q + 1) * PC].rearrange(
                        "p (a b) -> p a b", b=BH),
                    ind_b, ALU.mult).then_inc(h2, 1)
            emv(fwm)
            c_dv.c += 1
            if q == SPLIT - 1:
                dv_mul[j] = c_dv.c

        dv_bb_lastuse = {}

        def emit_bias_load(m, j):
            slot = j % 2
            src = bias_r if m == 0 else bbase_r
            key = (m, slot, j - 2)
            if key in dv_bb_lastuse:
                wait("gpsimd", c_dv, dv_bb_lastuse[key])
            dma("gpsimd", bb_sb[m][slot][:, :],
                src[0:1, j * NC:(j + 1) * NC].broadcast_to((P, NC)), s_bb[m][slot])

        NJR = NJ if NJ_RUN is None else NJ_RUN
        emit_m_dmas(0)
        emit_w_dmas(0)
        if NJR > 1:
            emit_m_dmas(1)
            emit_w_dmas(1)
        emit_wb_dmas(0)
        for jj in range(min(2, NJR)):
            emit_bias_load(0, jj)
            emit_bias_load(1, jj)
        for q in range(SPLIT):
            emit_reduce_piece(0, q)
        emit_bs16(0)
        emit_bd(0)
        emit_min(0)
        for q in range(SPLIT):
            emit_mul_piece(0, q)
        if NJR > 1:
            for q in range(SPLIT):
                emit_reduce_piece(1, q)
            emit_bs16(1)
            emit_bd(1)
            emit_min(1)
        if NJR > 2:
            emit_m_dmas(2)
            emit_w_dmas(2)

        CH = [(m, j) for j in range(NJR) for m in (0, 1)]
        dv_sevac = {}
        dv_fevac = {}
        dv_epi = {}
        ac_silu_hist = {}
        dv_xr_lastuse = {}
        ob_round = [0, 0]
        xr_round = [0, 0]

        for g, (m, j) in enumerate(CH):
            if m == 0:
                wait("tensor", c_dv, dv_mul[j])
            else:
                wait("tensor", s_wbL, wb_cnt[j])
                if j == 0:
                    wait("tensor", c_dv, dv_aT_done)
            slot = j % 2
            for t in range(TT):
                pp = t % 2
                if m == 0:
                    ps = ps_s[pp]
                    prevkey = (t - 2, j) if t >= 2 else (TT - 2 + t, j - 1)
                    if prevkey in dv_sevac:
                        wait("tensor", c_dv, dv_sevac[prevkey])
                    if j > 0 and (t, j - 1) in dv_epi:
                        wait("tensor", c_dv, dv_epi[(t, j - 1)])
                else:
                    ps = ps_f[pp]
                    prevkey = (t - 2, j) if t >= 2 else (TT - 2 + t, j - 1)
                    if prevkey in dv_fevac:
                        wait("tensor", c_dv, dv_fevac[prevkey])
                    if t == 1 and j + 2 < NJR:
                        emit_bd(j + 2)
                for kc in range(KC):
                    stop = (kc == KC - 1 and m == 0)
                    def fmm(te, kc=kc, t=t, ps=ps, strip=wT_sb[m],
                            start=(kc == 0), stop=stop, h3=c_pe.h):
                        ins = te.matmul(
                            ps[:, 0:NC],
                            xnTt_sb[t][:, kc * P:(kc + 1) * P],
                            strip[:, kc * NC:(kc + 1) * NC],
                            start=start, stop=stop)
                        if stop:
                            ins.then_inc(h3, 1)
                    em("tensor", fmm)
                if m == 1:
                    def flr(te, t=t, ps=ps, j=j, h3=c_pe.h):
                        te.matmul(ps[:, 0:NC],
                                  aT_sb[:, t * P:(t + 1) * P],
                                  BpT_sb[:, j * NC:(j + 1) * NC],
                                  start=False, stop=True).then_inc(h3, 1)
                    em("tensor", flr)
                c_pe.c += 1
                pe_tile = c_pe.c

                if m == 0:
                    wait("vector", c_pe, pe_tile)
                    wait("vector", s_bb[0][slot], s_bb[0][slot].c)
                    def fse(v, t=t, pp=pp, slot=slot, h2=c_dv.h):
                        v.tensor_tensor(s_sb[:, t * NC:(t + 1) * NC],
                                        ps_s[pp][:, 0:NC], bb_sb[0][slot][:, :],
                                        ALU.add).then_inc(h2, 1)
                    emv(fse)
                    c_dv.c += 1
                    dv_sevac[(t, j)] = c_dv.c
                    dv_bb_lastuse[(0, slot, j)] = c_dv.c
                    if j + 2 < NJR and t >= TT - SPLIT:
                        emit_reduce_piece(j + 2, t - (TT - SPLIT))
                        if t == TT - 1:
                            emit_bs16(j + 2)
                else:
                    epi = j * TT + t
                    pslot = epi % 2
                    key = (pslot, xr_round[pslot] - 1)
                    if key in dv_xr_lastuse:
                        wait("scalar", c_dv, dv_xr_lastuse[key])
                    dma("scalar", xr_sb[pslot][:, :],
                        x_sh[t * P:(t + 1) * P, j * NC:(j + 1) * NC], s_xr[pslot])
                    xrv = s_xr[pslot].c
                    xr_round[pslot] += 1
                    wait("vector", c_pe, pe_tile)
                    wait("vector", s_bb[1][slot], s_bb[1][slot].c)
                    if (epi - 2) in ac_silu_hist:
                        wait("vector", c_ac, ac_silu_hist[epi - 2])
                    def ffe(v, pp=pp, slot=slot, pslot=pslot, h2=c_dv.h):
                        v.tensor_tensor(fg_sb[pslot][:, :], ps_f[pp][:, 0:NC],
                                        bb_sb[1][slot][:, :], ALU.add).then_inc(h2, 1)
                    emv(ffe)
                    c_dv.c += 1
                    dv_fevac[(t, j)] = c_dv.c
                    dv_bb_lastuse[(1, slot, j)] = c_dv.c
                    wait("scalar", c_dv, dv_fevac[(t, j)])
                    ems(lambda sc, pslot=pslot, h4=c_ac.h: sc.activation(
                        g_sb[pslot][:, :], fg_sb[pslot][:, :],
                        AF.Silu).then_inc(h4, 1))
                    c_ac.c += 1
                    ac_silu_hist[epi] = c_ac.c
                    wait("vector", c_ac, ac_silu_hist[epi])
                    def fp(v, t=t, pslot=pslot, h2=c_dv.h):
                        v.tensor_tensor(p_sb[pslot][:, :],
                                        s_sb[:, t * NC:(t + 1) * NC],
                                        g_sb[pslot][:, :], ALU.mult).then_inc(h2, 1)
                    emv(fp)
                    c_dv.c += 1
                    wait("vector", s_xr[pslot], xrv)
                    if ob_round[pslot] > 0:
                        wait("vector", s_ob[pslot], 16 * ob_round[pslot])
                    def fo(v, pslot=pslot, h2=c_dv.h):
                        v.tensor_tensor(ob_sb[pslot][:, :], p_sb[pslot][:, :],
                                        xr_sb[pslot][:, :], ALU.add).then_inc(h2, 1)
                    emv(fo)
                    c_dv.c += 1
                    dv_epi[(t, j)] = c_dv.c
                    dv_xr_lastuse[(pslot, xr_round[pslot] - 1)] = c_dv.c
                    wait("scalar", c_dv, dv_epi[(t, j)])
                    dma("scalar", out_sh[t * P:(t + 1) * P, j * NC:(j + 1) * NC],
                        ob_sb[pslot][:, :], s_ob[pslot])
                    ob_round[pslot] += 1
                    if t == 1 and j + 2 < NJR:
                        emit_min(j + 2)
                    if t < SPLIT and j + 1 < NJR:
                        emit_mul_piece(j + 1, t)
            pe_chunk_done[g] = c_pe.c
            if m == 1:
                if j + 1 < NJR:
                    emit_wb_dmas(j + 1)
                if j + 2 < NJR:
                    emit_bias_load(0, j + 2)
                    emit_bias_load(1, j + 2)
                if j + 3 < NJR:
                    emit_m_dmas(j + 3)
                    emit_w_dmas(j + 3)

        wait("scalar", s_ob[0], 16 * ob_round[0])
        wait("scalar", s_ob[1], 16 * ob_round[1])
        wait("gpsimd", s_ob[0], 16 * ob_round[0])
        wait("gpsimd", s_ob[1], 16 * ob_round[1])

        return _finish(nc, ops)


def _finish(nc, ops):
    with nc.Block() as block:
        @block.sync
        def _(e):
            for f in ops["sync"]:
                f(e)

        @block.gpsimd
        def _(e):
            for f in ops["gpsimd"]:
                f(e)

        @block.vector
        def _(e):
            for f in ops["vector"]:
                f(e)

        @block.scalar
        def _(e):
            for f in ops["scalar"]:
                f(e)

        @block.tensor
        def _(e):
            for f in ops["tensor"]:
                f(e)
    return nc


def _tile_kmajor(a, NJ, NC, KC, P):
    N, K = a.shape
    return np.ascontiguousarray(
        a.reshape(NJ, NC, KC, P).transpose(0, 2, 3, 1)).reshape(NJ * KC * P, NC)


def kernel(L0_attn_data, L0_attn_mask, L0_attn_weight, L0_attn_bias,
           L0_ffn_W_base, L0_ffn_b_base, L0_ffn_A, L0_ffn_B, L0_ffn_WL,
           L0_ffn_WR):
    D, S, NCORES, NC = 4096, 8192, 8, 256
    P, KC, NJ = 128, D // 128, D // NC
    T = S // NCORES
    x = np.ascontiguousarray(np.asarray(L0_attn_data, np.float32).reshape(S, D))
    w = np.asarray(L0_attn_weight, np.float32)
    mk = np.asarray(L0_attn_mask, np.int32)
    wb = np.asarray(L0_ffn_W_base, np.float32)
    wt = _tile_kmajor(w, NJ, NC, KC, P)
    mkt = _tile_kmajor(mk, NJ, NC, KC, P)
    wbt = _tile_kmajor(wb, NJ, NC, KC, P)
    bias = np.ascontiguousarray(np.asarray(L0_attn_bias, np.float32).reshape(1, D))
    bb = np.ascontiguousarray(np.asarray(L0_ffn_b_base, np.float32).reshape(1, D))
    A = np.ascontiguousarray(np.asarray(L0_ffn_A, np.float32))
    WL = np.ascontiguousarray(np.asarray(L0_ffn_WL, np.float32))
    B = np.ascontiguousarray(np.asarray(L0_ffn_B, np.float32))
    WR = np.ascontiguousarray(np.asarray(L0_ffn_WR, np.float32))

    nc = build_nc(D=D, T=T, NC=NC)
    in_maps = []
    for c in range(NCORES):
        in_maps.append(dict(
            x_sh=x[c * T:(c + 1) * T], wt_t=wt, mk_t=mkt, wbt_t=wbt,
            bias_r=bias, bbase_r=bb, A_r=A, WL_r=WL, B_r=B, WR_r=WR))
    res = run_bass_kernel_spmd(nc, in_maps, core_ids=list(range(NCORES)))
    out = np.concatenate([res.results[c]["out_sh"] for c in range(NCORES)], axis=0)
    return out.reshape(1, S, D)

